# version 15
# baseline (speedup 1.0000x reference)
import math
from contextlib import ExitStack

import numpy as np
import ml_dtypes

import concourse.bass as bass
import concourse.mybir as mybir
from concourse.bass_utils import run_bass_kernel_spmd

F32 = mybir.dt.float32
BF16 = mybir.dt.bfloat16
FP16 = mybir.dt.float16
I32 = mybir.dt.int32
AF = mybir.ActivationFunctionType
ALU = mybir.AluOpType
AX = mybir.AxisListType

D = 1024
SEQ = 2048
NTS = 16
NSEQ = 2
NT = NTS * NSEQ
H = 16
KVH = 4
HD = 64
HP = 8
LAT = 256
LN_EPS = 1e-5
QK_EPS = 1e-6

ALL_ENGINES = ("sync", "vector", "scalar", "tensor", "gpsimd")


class Sem:
    def __init__(self, handle, name, engine=None, is_dma=False):
        self.h = handle
        self.name = name
        self.n = 0
        self.engine = engine
        self.is_dma = is_dma

    def tick(self, k=1):
        self.n += k
        return (self, self.n)


class Prog:
    def __init__(self):
        self.steps = {e: [] for e in ALL_ENGINES}

    def step(self, eng, fn, waits=(), inc=None):
        tok = None
        inc_t = None
        if inc is not None:
            sem, amt = inc if isinstance(inc, tuple) else (inc, 1)
            tok = sem.tick(amt)
            inc_t = (sem, amt)
        ws = [w for w in waits if w is not None]
        self.steps[eng].append((fn, ws, inc_t))
        return tok

    def barrier(self, eng, waits, inc):
        return self.step(eng, lambda e: e.engine_nop(), waits=waits, inc=inc)

    def emit(self, block):
        def make(eng_name):
            def body(e):
                seen = {}
                for fn, waits, inc_t in self.steps[eng_name]:
                    for sem, val in waits:
                        if (sem.engine == eng_name and not sem.is_dma
                                and eng_name == "tensor"):
                            continue
                        if seen.get(sem, -1) >= val:
                            continue
                        e.wait_ge(sem.h, val)
                        seen[sem] = val
                    ins = fn(e)
                    if inc_t is not None:
                        sem, amt = inc_t
                        ins.then_inc(sem.h, amt)
            return body
        block.sync(make("sync"))
        block.vector(make("vector"))
        block.scalar(make("scalar"))
        block.tensor(make("tensor"))
        block.gpsimd(make("gpsimd"))

    def check(self):
        counts = {}
        idx = {e: 0 for e in ALL_ENGINES}
        total = sum(len(v) for v in self.steps.values())
        done = 0
        progress = True
        while progress:
            progress = False
            for e in ALL_ENGINES:
                while idx[e] < len(self.steps[e]):
                    fn, waits, inc_t = self.steps[e][idx[e]]
                    if any(counts.get(s, 0) < v for s, v in waits
                           if not (s.engine == e and not s.is_dma
                                   and e == "tensor")):
                        break
                    if inc_t is not None:
                        s, amt = inc_t
                        counts[s] = counts.get(s, 0) + amt
                    idx[e] += 1
                    done += 1
                    progress = True
        if done != total:
            msgs = []
            for e in ALL_ENGINES:
                if idx[e] < len(self.steps[e]):
                    fn, waits, inc_t = self.steps[e][idx[e]]
                    bad = [(s.name, v, counts.get(s, 0)) for s, v in waits
                           if counts.get(s, 0) < v]
                    msgs.append(f"{e}@{idx[e]}/{len(self.steps[e])}: {bad}")
            raise RuntimeError("DEADLOCK " + " | ".join(msgs))


def ts(i, n):
    return slice(i * n, (i + 1) * n)


def max_toks(toks):
    best = {}
    for t in toks:
        if t is None:
            continue
        s, v = t
        if s not in best or v > best[s]:
            best[s] = v
    return [(s, v) for s, v in best.items()]


def build(debug=(), phases=("p0", "p1", "p2", "p3", "p5")):
    nc = bass.Bass("TRN2", target_bir_lowering=False)
    dt = nc.dram_tensor

    x_in = dt("x", [NSEQ, SEQ, D], F32, kind="ExternalInput").ap()
    out_t = dt("out", [NSEQ, SEQ, D], F32, kind="ExternalOutput").ap()
    wq_g = dt("wq_g", [D, D], F32, kind="ExternalInput").ap()
    kvd = dt("kvd", [D, LAT], F32, kind="ExternalInput").ap()
    kup = dt("kup", [LAT, LAT], F32, kind="ExternalInput").ap()
    vup = dt("vup", [LAT, LAT], F32, kind="ExternalInput").ap()
    gwo = dt("gwo", [D, D], F32, kind="ExternalInput").ap()
    lwq = dt("lwq", [D, D], F32, kind="ExternalInput").ap()
    lwk = dt("lwk", [D, LAT], F32, kind="ExternalInput").ap()
    lwv = dt("lwv", [D, LAT], F32, kind="ExternalInput").ap()
    lwo = dt("lwo", [D, D], F32, kind="ExternalInput").ap()
    nw1T = dt("nw1T", [128, 8], F32, kind="ExternalInput").ap()
    nw2T = dt("nw2T", [128, 8], F32, kind="ExternalInput").ap()
    wfcT = dt("wfcT", [D, 4096], F32, kind="ExternalInput").ap()
    wprT = dt("wprT", [4096, D], F32, kind="ExternalInput").ap()
    bfcT = dt("bfcT", [128, 32], F32, kind="ExternalInput").ap()
    bprR = dt("bprR", [1, D], F32, kind="ExternalInput").ap()
    afc = dt("afc", [16, D], F32, kind="ExternalInput").ap()
    bfcL = dt("bfcL", [16, 4096], F32, kind="ExternalInput").ap()
    apr = dt("apr", [16, 4096], F32, kind="ExternalInput").ap()
    bprL = dt("bprL", [16, D], F32, kind="ExternalInput").ap()
    cosg = dt("cosg", [128, 512], F32, kind="ExternalInput").ap()
    sing = dt("sing", [128, 512], F32, kind="ExternalInput").ap()
    cosl = dt("cosl", [128, 512], F32, kind="ExternalInput").ap()
    sinl = dt("sinl", [128, 512], F32, kind="ExternalInput").ap()

    s_scr = dt("s_scr", [NT * 128, D], FP16, kind="Internal").ap()
    xp_scr = dt("xp_scr", [2, 128], F32, kind="Internal").ap()
    w1_scr = dt("w1_scr", [D, 4096], BF16, kind="Internal").ap()
    w2_scr = dt("w2_scr", [4096, D], BF16, kind="Internal").ap()

    dbg = {}
    def dbg_t(name, shape, dtype=F32):
        if name in debug:
            dbg[name] = dt("dbg_" + name, shape, dtype, kind="ExternalOutput").ap()
        return dbg.get(name)

    dbg_t("xn", [NT * 128, D], BF16)
    dbg_t("x1", [NT * 128, D], BF16)
    dbg_t("s", [NT * 128, D], FP16)
    dbg_t("kt", [128, 8192], BF16)
    dbg_t("vt", [128, 8192], BF16)
    dbg_t("qt", [128, 4096], BF16)
    dbg_t("w1", [D, 4096], BF16)
    dbg_t("w2", [4096, D], BF16)
    dbg_t("h0", [128, 8192], BF16)
    dbg_t("stats", [128, 128], F32)
    dbg_t("smf", [128, 64], F32)

    with ExitStack() as ctx:
        E = ctx.enter_context

        arA = E(nc.sbuf_tensor("arA", [128, 32768], BF16))
        arB = E(nc.sbuf_tensor("arB", [128, 32768], BF16))
        wqs = E(nc.sbuf_tensor("wqs", [128, 8192], BF16))
        wos = E(nc.sbuf_tensor("wos", [128, 8192], BF16))
        wkv = E(nc.sbuf_tensor("wkv", [128, 4096], BF16))
        scr1 = E(nc.sbuf_tensor("scr1", [128, 1024], F32))
        scr2 = E(nc.sbuf_tensor("scr2", [128, 512], F32))
        xi = E(nc.sbuf_tensor("xi", [128, 2048], F32))
        ssb = E(nc.sbuf_tensor("ssb", [128, 2048], FP16))
        wp_i = E(nc.sbuf_tensor("wp_i", [128, 512], I32))
        wp_c = E(nc.sbuf_tensor("wp_c", [128, 512], F32))
        wp_q = E(nc.sbuf_tensor("wp_q", [128, 512], BF16))
        rz = E(nc.sbuf_tensor("rz", [64, 512], F32))
        rz2 = E(nc.sbuf_tensor("rz2", [64, 512], F32))
        rcos = E(nc.sbuf_tensor("rcos", [128, 512], F32))
        rsin = E(nc.sbuf_tensor("rsin", [128, 512], F32))
        stats = E(nc.sbuf_tensor("stats", [128, 128], F32))
        bfcs = E(nc.sbuf_tensor("bfcs", [128, 32], F32))
        bprb = E(nc.sbuf_tensor("bprb", [128, 1024], FP16))
        nws = E(nc.sbuf_tensor("nws", [128, 16], F32))
        kbf = E(nc.sbuf_tensor("kbf", [128, 256], BF16))
        latb = E(nc.sbuf_tensor("latb", [128, 256], BF16))
        smf = E(nc.sbuf_tensor("smf", [128, 64], F32))
        onesf = E(nc.sbuf_tensor("onesf", [1, 128], F32))
        zcol = E(nc.sbuf_tensor("zcol", [1, 128], BF16))
        wsq = E(nc.sbuf_tensor("wsq", [128, 64], F32))
        rowf = E(nc.sbuf_tensor("rowf", [1, 260], F32))
        afcs = E(nc.sbuf_tensor("afcs", [16, 128], BF16))
        aprs = E(nc.sbuf_tensor("aprs", [16, 128], BF16))
        bls = E(nc.sbuf_tensor("bls", [16, 512], BF16))
        idn = E(nc.sbuf_tensor("idn", [128, 128], BF16))

        B_KT, B_V, B_QT, B_AO, B_QB, B_EB, B_XT = \
            0, 8192, 16384, 24576, 28672, 29696, 31744

        PS_S0 = E(nc.psum_tensor("PS_S0", [128, 1024], F32))
        PS_S1 = E(nc.psum_tensor("PS_S1", [128, 1024], F32))
        PS_O0 = E(nc.psum_tensor("PS_O0", [128, 512], F32))
        PS_O1 = E(nc.psum_tensor("PS_O1", [128, 512], F32))
        PS_W = E(nc.psum_tensor("PS_W", [128, 512], F32))
        PS_TR = E(nc.psum_tensor("PS_TR", [128, 1024], BF16))

        sP = Sem(E(nc.semaphore("sP")), "sP", "tensor")
        sV = Sem(E(nc.semaphore("sV")), "sV", "vector")
        sA = Sem(E(nc.semaphore("sA")), "sA", "scalar")
        sM = Sem(E(nc.semaphore("sM")), "sM", "gpsimd")
        lanes = {"sync": [Sem(E(nc.semaphore(f"sDL{i}")), f"sDL{i}", "sync",
                              is_dma=True) for i in range(14)],
                 "gpsimd": [Sem(E(nc.semaphore(f"sGL{i}")), f"sGL{i}",
                                "gpsimd", is_dma=True) for i in range(6)]}
        lane_i = {"sync": 0, "gpsimd": 0}
        sE = {k: Sem(E(nc.semaphore("sE" + k[0])), "sE" + k[0], k)
              for k in ("vector", "scalar", "gpsimd")}
        eN = {k: [0] for k in sE}

        def ed(e, ins, eng="vector"):
            c = eN[eng]
            c[0] += 1
            ins.then_inc(sE[eng].h, 1)
            e.wait_ge(sE[eng].h, c[0])
            return ins

        prog = Prog()

        def dma(eng, dst, src, waits=(), **kw):
            ls = lanes[eng]
            L = ls[lane_i[eng] % len(ls)]
            lane_i[eng] += 1
            pre = (L, L.n) if L.n > 0 else None
            return prog.step(eng,
                             lambda e: e.dma_start(out=dst, in_=src, **kw),
                             waits=list(waits) + [pre], inc=(L, 16))

        def all_dma_done(eng=None):
            toks = []
            for k, ls in lanes.items():
                if eng is None or k == eng:
                    toks += [(L, L.n) for L in ls if L.n > 0]
            return toks

        t_nw = dma("sync", nws[:, 0:8], nw1T[:, :])
        t_nw2 = dma("sync", nws[:, 8:16], nw2T[:, :])
        dma("sync", bfcs[:, :], bfcT[:, :])
        t_rope = [dma("sync", rcos[:, :], cosg[:, :]),
                  dma("sync", rsin[:, :], sing[:, :])]


        def gp_setup(g):
            ed(g, g.memset(idn[:], 0.0), "gpsimd")
            g.affine_select(out=idn[:], in_=idn[:], compare_op=ALU.not_equal,
                            fill=1.0, base=0, pattern=[[-1, 128]],
                            channel_multiplier=1)
            g.memset(onesf[:, :], 1.0)
            g.memset(zcol[:, :], 0.0)
            v3 = arB[:, B_V:B_V + 8192].rearrange("p (k c) -> p k c", c=128)
            g.memset(v3[:, :, 64:128], 1.0)
            g.memset(smf[:, 47:48], LN_EPS)
            return g.memset(smf[:, 48:49], QK_EPS)
        t_gpset = prog.step("gpsimd", gp_setup, inc=sM)

        def w_chunked(src):
            return src.rearrange("(c p) n -> p c n", p=128)

        def dma_w(dst2d, src, waits=(), ncol=None):
            s3 = src.rearrange("(c p) n -> p c n", p=128)
            n = s3.shape[2]
            d3 = dst2d.rearrange("p (c n) -> p c n", n=n)
            return dma("gpsimd", d3, s3, waits=waits)

        def load_glob_weights(waits=()):
            tw0 = dma_w(wqs[:, :], wq_g, waits)
            tw1 = dma_w(wkv[:, 0:2048], kvd, waits)
            tw2 = dma_w(wkv[:, 2048:2560], kup, waits)
            tw3 = dma_w(wkv[:, 2560:3072], vup, waits)
            tw4 = dma_w(wos[:, :], gwo, waits)
            def scale_rows(v):
                last = None
                for ic in range(8):
                    last = v.tensor_scalar(wqs[:, ts(ic, 1024)],
                                           wqs[:, ts(ic, 1024)],
                                           nws[:, ic:ic + 1], None, ALU.mult)
                for ic in range(8):
                    last = v.tensor_scalar(wkv[:, ts(ic, 256)],
                                           wkv[:, ts(ic, 256)],
                                           nws[:, ic:ic + 1], None, ALU.mult)
                return last
            tq = prog.step("vector", scale_rows, waits=[tw0, tw1, t_nw], inc=sV)
            return {"wq": tq, "wkv": max_toks([tw2, tw3, tq]),
                    "wo": tw4, "all": max_toks([tq, tw2, tw3, tw4])}

        def load_loc_weights(waits=()):
            tw0 = dma_w(wqs[:, :], lwq, waits)
            tw1 = dma_w(wkv[:, 0:2048], lwk, waits)
            tw2 = dma_w(wkv[:, 2048:4096], lwv, waits)
            tw3 = dma_w(wos[:, :], lwo, waits)
            def scale_rows(v):
                last = None
                for ic in range(8):
                    last = v.tensor_scalar(wqs[:, ts(ic, 1024)],
                                           wqs[:, ts(ic, 1024)],
                                           nws[:, 8 + ic:9 + ic], None, ALU.mult)
                for ic in range(16):
                    last = v.tensor_scalar(wkv[:, ts(ic, 256)],
                                           wkv[:, ts(ic, 256)],
                                           nws[:, 8 + ic % 8:9 + ic % 8],
                                           None, ALU.mult)
                return last
            tq = prog.step("vector", scale_rows,
                           waits=[tw0, tw1, tw2, t_nw2], inc=sV)
            return {"wq": tq, "wkv": [tq], "wo": tw3,
                    "all": max_toks([tq, tw3])}

        def ps_init(v):
            v.memset(PS_S0[:, :], 0.0)
            return v.memset(PS_S1[:, :], 0.0)
        t_psinit = prog.step("vector", ps_init, inc=sV)

        g_psw = [t_gpset]
        g_pso = [None, None]
        g_pss = [t_psinit, t_psinit]
        g_ptr = [None]
        g_eb = [None, None]
        g_qt = [None, None]
        g_ao = [None]
        g_xi = [None, None]
        g_ssb = [None, None]
        g_wsq = [None]
        tok_xn = [None] * NT
        t_wq_rd = [None]
        t_wkv_rd = [None]
        t_wo_rd = [None]
        t_rope_rd = [None]

        def emit_wbuild(tt, waits=()):
            def wb(v, tt=tt):
                c = rcos[:, ts(tt, 32)]
                s = rsin[:, ts(tt, 32)]
                v.tensor_mul(wsq[:, 0:32], c, c)
                ed(v, v.tensor_mul(wsq[:, 32:64], s, s))
                return v.tensor_add(wsq[:, 0:32], wsq[:, 0:32], wsq[:, 32:64])
            return prog.step("vector", wb,
                             waits=list(waits) + t_rope + [g_wsq[0]], inc=sV)

        def wsq_view(nheads):
            return wsq[:, 0:32].rearrange("p (a k) -> p a k", a=1)\
                .rearrange("p a k -> p a b k" if False else "p a k -> p a k")

        def transpose_group(src_fn, dst_fn, nchunks, waits=(), identity=None,
                            pst=None, pwidth=128, ev_eng="vector"):
            identity = idn if identity is None else identity
            pst = PS_TR if pst is None else pst
            def tp(t):
                last = None
                for c in range(nchunks):
                    last = t.transpose(pst[:, c * pwidth:(c + 1) * pwidth],
                                       src_fn(c), identity[:])
                return last
            ptok = prog.step("tensor", tp, waits=list(waits) + [g_ptr[0]], inc=sP)
            def ev(e):
                last = None
                for c in range(nchunks):
                    if ev_eng == "vector":
                        last = e.tensor_copy(dst_fn(c),
                                             pst[:, c * pwidth:(c + 1) * pwidth])
                    else:
                        last = e.copy(dst_fn(c),
                                      pst[:, c * pwidth:(c + 1) * pwidth])
                return last
            etok = prog.step(ev_eng, ev, waits=[ptok],
                             inc=sV if ev_eng == "vector" else sA)
            g_ptr[0] = etok
            return ptok, etok

        import os as _os
        _p0n = int(_os.environ.get("P0N", NT))
        if "p0" in phases:
            for T in range(_p0n):
                sq_i, t_i = divmod(T, NTS)
                slot = T % 2
                tld = dma("sync", xi[:, ts(slot, 1024)],
                          x_in[sq_i, ts(t_i, 128), :], waits=[g_xi[slot]])
                def p0_stats(s, slot=slot):
                    return s.activation(scr1[:, :], xi[:, ts(slot, 1024)],
                                        AF.Square, accum_out=smf[:, 49:50])
                ta0 = prog.step("scalar", p0_stats, waits=[tld, t_gpset],
                                inc=sA)
                def p0_stats2(s, slot=slot):
                    return s.activation(scr1[:, :], xi[:, ts(slot, 1024)],
                                        AF.Identity, accum_out=smf[:, 46:47])
                ta = prog.step("scalar", p0_stats2, waits=[ta0], inc=sA)
                def p0_small(v, T=T):
                    ed(v, v.tensor_scalar(stats[:, 4 * T:4 * T + 1],
                                          smf[:, 46:47], 1.0 / D, None,
                                          ALU.mult))
                    v.tensor_scalar(smf[:, 50:51], smf[:, 49:50], 1.0 / D,
                                    None, ALU.mult)
                    ed(v, v.tensor_mul(smf[:, 51:52],
                                       stats[:, 4 * T:4 * T + 1],
                                       stats[:, 4 * T:4 * T + 1]))
                    return v.tensor_sub(smf[:, 50:51], smf[:, 50:51],
                                        smf[:, 51:52])
                tv = prog.step("vector", p0_small, waits=[ta], inc=sV)
                def p0_ln(s, T=T):
                    return s.activation(smf[:, 51:52], smf[:, 50:51], AF.Ln,
                                        bias=smf[:, 47:48])
                ta1 = prog.step("scalar", p0_ln, waits=[tv], inc=sA)
                def p0_exp(s, T=T):
                    return s.activation(stats[:, 4 * T + 1:4 * T + 2],
                                        smf[:, 51:52], AF.Exp, scale=-0.5)
                ta2 = prog.step("scalar", p0_exp, waits=[ta1], inc=sA)
                def p0_mb(v, T=T):
                    return v.tensor_scalar(smf[:, 46:47],
                                           stats[:, 4 * T:4 * T + 1],
                                           stats[:, 4 * T + 1:4 * T + 2], -1.0,
                                           ALU.mult, ALU.mult)
                tv2 = prog.step("vector", p0_mb, waits=[ta2], inc=sV)
                def p0_apply(s, T=T, slot=slot):
                    return s.activation(arA[:, ts(T, 1024)],
                                        xi[:, ts(slot, 1024)], AF.Identity,
                                        scale=stats[:, 4 * T + 1:4 * T + 2],
                                        bias=smf[:, 46:47])
                tok_xn[T] = prog.step("scalar", p0_apply, waits=[tv2], inc=sA)
                g_xi[slot] = tok_xn[T]
            if "xn" in dbg:
                for T in range(_p0n):
                    dma("sync", dbg["xn"][ts(T, 128), :], arA[:, ts(T, 1024)],
                        waits=[tok_xn[T]])
            if "stats" in dbg:
                dma("sync", dbg["stats"][:, :], stats[:, :],
                    waits=[tok_xn[_p0n - 1]])
            if "smf" in dbg:
                dma("sync", dbg["smf"][:, :], smf[:, :],
                    waits=[tok_xn[_p0n - 1]])

        wprep_units = []
        wprep_last_store = [None]

        def wprep_build():
            units = []
            def init_amax(v):
                v.memset(smf[:, 44:45], 0.0)
                return v.memset(smf[:, 45:46], 0.0)
            units.append(lambda: prog.step("vector", init_amax, inc=sV))
            blocks = [(wfcT, 0, r, c) for r in range(0, D, 128)
                      for c in range(0, 4096, 1024)] + \
                     [(wprT, 1, r, 0) for r in range(0, 4096, 128)]
            for src, wi, r, c in blocks:
                def unit(src=src, wi=wi, r=r, c=c):
                    slot = (r // 128 + c // 1024) % 2
                    tl = dma("sync", xi[:, ts(slot, 1024)],
                             src[r:r + 128, c:c + 1024], waits=[g_xi[slot]])
                    def red(v, slot=slot, wi=wi):
                        ed(v, v.tensor_reduce(smf[:, 53:54],
                                              xi[:, ts(slot, 1024)],
                                              AX.X, ALU.max,
                                              apply_absolute_value=True))
                        ed(v, v.tensor_max(smf[:, 44 + wi:45 + wi],
                                           smf[:, 44 + wi:45 + wi],
                                           smf[:, 53:54]))
                        return v.engine_nop()
                    g_xi[slot] = prog.step("vector", red, waits=[tl], inc=sV)
                units.append(unit)
            scl_done = [None]
            def xp_unit():
                ta = dma("sync", xp_scr[0, :], smf[:, 44:45],
                         waits=[g_xi[0], g_xi[1]])
                tb = dma("sync", xp_scr[1, :], smf[:, 45:46])
                t1 = dma("sync", rowf[0:1, 0:256],
                         xp_scr.rearrange("a b -> (a b)"),
                         waits=[ta, tb])
                t2 = t1
                def mx(v):
                    v.tensor_reduce(rowf[0:1, 256:257], rowf[0:1, 0:128],
                                    AX.X, ALU.max)
                    ed(v, v.tensor_reduce(rowf[0:1, 258:259],
                                          rowf[0:1, 128:256], AX.X, ALU.max))
                    v.tensor_scalar(rowf[0:1, 256:257], rowf[0:1, 256:257],
                                    1.0 / 127, None, ALU.mult)
                    ed(v, v.tensor_scalar(rowf[0:1, 258:259],
                                          rowf[0:1, 258:259], 1.0 / 127,
                                          None, ALU.mult))
                    v.reciprocal(rowf[0:1, 257:258], rowf[0:1, 256:257])
                    return v.reciprocal(rowf[0:1, 259:260],
                                        rowf[0:1, 258:259])
                tv = prog.step("vector", mx, waits=[t1, t2], inc=sV)
                def bc(t):
                    return t.matmul(PS_W[:, 0:4], onesf[0:1, :],
                                    rowf[0:1, 256:260], start=True, stop=True)
                tp = prog.step("tensor", bc, waits=[tv, g_psw[0]], inc=sP)
                def cp(v):
                    return v.tensor_copy(smf[:, 40:44], PS_W[:, 0:4])
                g_psw[0] = prog.step("vector", cp, waits=[tp], inc=sV)
                scl_done[0] = g_psw[0]
            units.append(xp_unit)
            g_wpq = [None]
            scl_tok_ref = scl_done
            for src, wi, r, c in blocks:
                def unit(src=src, wi=wi, r=r, c=c):
                    scrd = w1_scr if wi == 0 else w2_scr
                    slot = (r // 128 + c // 1024) % 2
                    tl = dma("sync", xi[:, ts(slot, 1024)],
                             src[r:r + 128, c:c + 1024], waits=[g_xi[slot]])
                    icol = 41 + 2 * wi
                    sccol = 40 + 2 * wi
                    lastst = None
                    for half in range(2):
                        def quant_a(v, slot=slot, half=half, icol=icol):
                            xs = xi[:, slot * 1024 + half * 512:
                                    slot * 1024 + half * 512 + 512]
                            ed(v, v.tensor_scalar(wp_i[:, :], xs,
                                                  smf[:, icol:icol + 1],
                                                  None, ALU.mult))
                            ed(v, v.scalar_tensor_tensor(
                                xs, xs, smf[:, icol:icol + 1], wp_i[:, :],
                                ALU.mult, ALU.subtract))
                            ed(v, v.scalar_tensor_tensor(
                                wp_c[:, :], xs, 0.5, wp_i[:, :], ALU.is_ge,
                                ALU.add))
                            ed(v, v.tensor_scalar(xs, xs, -0.5, None,
                                                  ALU.is_le))
                            return v.tensor_sub(wp_c[:, :], wp_c[:, :], xs)
                        tqa = prog.step("vector", quant_a,
                                        waits=[tl, scl_tok_ref[0]], inc=sV)
                        def quant_b(v, sccol=sccol):
                            return v.tensor_scalar(wp_q[:, :], wp_c[:, :],
                                                   smf[:, sccol:sccol + 1],
                                                   None, ALU.mult)
                        tq = prog.step("vector", quant_b,
                                       waits=[tqa, g_wpq[0]], inc=sV)
                        tst = dma("sync",
                                  scrd[r:r + 128, c + half * 512:
                                       c + half * 512 + 512],
                                  wp_q[:, :], waits=[tq])
                        g_wpq[0] = tst
                        lastst = tst
                    g_xi[slot] = lastst
                    wprep_last_store[0] = None
                units.append(unit)
            return units

        if "p5" in phases:
            wprep_units = wprep_build()
        wprep_pos = [0]

        def wprep_pump(n=1):
            for _ in range(n):
                if wprep_pos[0] < len(wprep_units):
                    wprep_units[wprep_pos[0]]()
                    wprep_pos[0] += 1

        def wprep_finish():
            while wprep_pos[0] < len(wprep_units):
                wprep_units[wprep_pos[0]]()
                wprep_pos[0] += 1
            return all_dma_done("sync")

        def emit_kv_build(sq_i, is_glob, wset):
            toks = []
            st = {}

            def uA(tt):
                T = sq_i * NTS + tt
                col = 2 if is_glob else 3
                kps = PS_O0[:, 0:256] if tt % 2 == 0 else PS_S0[:, 0:256]
                vps = PS_O1[:, 0:256] if tt % 2 == 0 else PS_S1[:, 0:256]
                g_k = g_pso if tt % 2 == 0 else g_pss
                def kv_ss(s, T=T):
                    return s.activation(scr1[:, :], arA[:, ts(T, 1024)],
                                        AF.Square, accum_out=smf[:, 49:50])
                ta = prog.step("scalar", kv_ss,
                               waits=[tok_xn[T], g_scr1g[0]], inc=sA)
                g_scr1g[0] = ta
                def kv_ln(s):
                    return s.activation(smf[:, 50:51], smf[:, 49:50], AF.Ln,
                                        scale=1.0 / D, bias=smf[:, 48:49])
                t_rs0 = prog.step("scalar", kv_ln, waits=[ta], inc=sA)
                def kv_exp(s, T=T, col=col):
                    return s.activation(stats[:, 4 * T + col:4 * T + col + 1],
                                        smf[:, 50:51], AF.Exp, scale=-0.5)
                t_rs = prog.step("scalar", kv_exp, waits=[t_rs0], inc=sA)
                _, t_xnT = transpose_group(
                    lambda c, T=T: arA[:, T * 1024 + c * 128:
                                       T * 1024 + (c + 1) * 128],
                    lambda c: arB[:, B_XT + c * 128:B_XT + (c + 1) * 128],
                    8, waits=[tok_xn[T]], ev_eng="scalar")
                if is_glob:
                    def lat(t):
                        last = None
                        for ic in range(8):
                            last = t.matmul(
                                PS_W[:, 0:256],
                                arB[:, B_XT + ic * 128:B_XT + (ic + 1) * 128],
                                wkv[:, ts(ic, 256)],
                                start=(ic == 0), stop=(ic == 7))
                        return last
                    t_lat = prog.step("tensor", lat,
                                      waits=[t_xnT, g_psw[0]] + wset["wkv"],
                                      inc=sP)
                    t_wkv_rd[0] = t_lat
                    def latev(s, T=T):
                        return s.activation(latb[:, 0:256], PS_W[:, 0:256],
                                            AF.Identity,
                                            scale=stats[:, 4 * T + 2:
                                                        4 * T + 3])
                    t_latev = prog.step("scalar", latev, waits=[t_lat, t_rs],
                                        inc=sA)
                    g_psw[0] = t_latev
                    _, t_latT = transpose_group(
                        lambda c: latb[:, c * 128:(c + 1) * 128],
                        lambda c: arB[:, B_QB + c * 128:B_QB + (c + 1) * 128],
                        2, waits=[t_latev])
                    def kmm(t, kps=kps):
                        last = None
                        for lc in range(2):
                            last = t.matmul(
                                kps,
                                arB[:, B_QB + lc * 128:B_QB + (lc + 1) * 128],
                                wkv[:, 2048 + lc * 256:2048 + (lc + 1) * 256],
                                start=(lc == 0), stop=(lc == 1))
                        return last
                    t_k = prog.step("tensor", kmm, waits=[t_latT, g_k[0]],
                                    inc=sP)
                    def vmm(t, vps=vps):
                        last = None
                        for lc in range(2):
                            last = t.matmul(
                                vps,
                                arB[:, B_QB + lc * 128:B_QB + (lc + 1) * 128],
                                wkv[:, 2560 + lc * 256:2560 + (lc + 1) * 256],
                                start=(lc == 0), stop=(lc == 1))
                        return last
                    t_v = prog.step("tensor", vmm, waits=[t_latT, g_k[1]],
                                    inc=sP)
                    t_wkv_rd[0] = t_v
                else:
                    def kmm(t, kps=kps):
                        last = None
                        for ic in range(8):
                            last = t.matmul(
                                kps,
                                arB[:, B_XT + ic * 128:B_XT + (ic + 1) * 128],
                                wkv[:, ts(ic, 256)],
                                start=(ic == 0), stop=(ic == 7))
                        return last
                    t_k = prog.step("tensor", kmm,
                                    waits=[t_xnT, g_k[0]] + wset["wkv"],
                                    inc=sP)
                    def vmm(t, vps=vps):
                        last = None
                        for ic in range(8):
                            last = t.matmul(
                                vps,
                                arB[:, B_XT + ic * 128:B_XT + (ic + 1) * 128],
                                wkv[:, 2048 + ic * 256:2048 + (ic + 1) * 256],
                                start=(ic == 0), stop=(ic == 7))
                        return last
                    t_v = prog.step("tensor", vmm,
                                    waits=[t_xnT, g_k[1]] + wset["wkv"],
                                    inc=sP)
                    t_wkv_rd[0] = t_v
                st[tt] = (T, kps, vps, g_k, t_k, t_v, t_rs)

            def uB(tt):
                T, kps, vps, g_k, t_k, t_v, t_rs = st.pop(tt)
                t_wb = emit_wbuild(tt)
                col = 2 if is_glob else 3
                v_scale_col = None if is_glob else 3
                k_pre_col = None if is_glob else 3
                def vev(s, tt=tt, T=T, v_scale_col=v_scale_col, vps=vps):
                    dst = arB[:, B_V + tt * 512:B_V + (tt + 1) * 512]
                    d3 = dst.rearrange("p (k c) -> p k c", c=128)[:, :, 0:64]
                    s3 = vps.rearrange("p (k c) -> p k c", c=64)
                    if v_scale_col is None:
                        return s.activation(d3, s3, AF.Identity)
                    return s.activation(d3, s3, AF.Identity,
                                        scale=stats[:, 4 * T + v_scale_col:
                                                    4 * T + v_scale_col + 1])
                t_vev = prog.step("scalar", vev, waits=[t_v, t_rs], inc=sA)
                g_k[1] = t_vev
                def ksq(s, kps=kps):
                    return s.activation(scr2[:, 0:256], kps, AF.Square)
                t_ksq = prog.step("scalar", ksq, waits=[t_k, g_scr2q[0]],
                                  inc=sA)
                def kwm(v):
                    s4 = scr2[:, 0:256].rearrange("p (g b k) -> p g b k",
                                                  g=4, b=2)
                    w4 = wsq[:, 0:32].rearrange("p (a b k) -> p a b k",
                                                a=1, b=1)\
                        .broadcast_to([128, 4, 2, 32])
                    ed(v, v.tensor_mul(s4, s4, w4))
                    return v.tensor_reduce(
                        smf[:, 32:36].rearrange("p (g o) -> p g o", o=1),
                        scr2[:, 0:256].rearrange("p (g k) -> p g k", k=64),
                        AX.X, ALU.add)
                t_kred = prog.step("vector", kwm, waits=[t_ksq, t_wb], inc=sV)
                g_scr2q[0] = t_kred
                g_wsq[0] = t_kred
                if k_pre_col is None:
                    def krs0(s):
                        return s.activation(smf[:, 36:40], smf[:, 32:36],
                                            AF.Ln, scale=1.0 / HD,
                                            bias=smf[:, 48:49])
                    tk0 = prog.step("scalar", krs0, waits=[t_kred], inc=sA)
                    def krs1(s):
                        return s.activation(smf[:, 36:40], smf[:, 36:40],
                                            AF.Exp, scale=-0.5)
                    t_krs = prog.step("scalar", krs1, waits=[tk0], inc=sA)
                else:
                    def krs_pre(v, T=T):
                        return v.tensor_scalar(
                            smf[:, 54:55],
                            stats[:, 4 * T + 3:4 * T + 4],
                            stats[:, 4 * T + 3:4 * T + 4],
                            1.0 / HD, ALU.mult, ALU.mult)
                    tk1 = prog.step("vector", krs_pre, waits=[t_rs], inc=sV)
                    def krsa(s):
                        return s.activation(smf[:, 36:40], smf[:, 32:36],
                                            AF.Ln, scale=smf[:, 54:55],
                                            bias=smf[:, 48:49])
                    tka = prog.step("scalar", krsa, waits=[tk1, t_kred],
                                    inc=sA)
                    def krsb(s):
                        return s.activation(smf[:, 36:40], smf[:, 36:40],
                                            AF.Exp, scale=-0.5)
                    t_krs = prog.step("scalar", krsb, waits=[tka], inc=sA)
                def krope(v, tt=tt, T=T, kps=kps, k_pre_col=k_pre_col):
                    p3 = kps.rearrange("p (g k) -> p g k", k=64)
                    pe, po = p3[:, :, 0:32], p3[:, :, 32:64]
                    c1 = rcos[:, ts(tt, 32)].rearrange("p (o k) -> p o k",
                                                       o=1)\
                        .broadcast_to([128, 4, 32])
                    s1 = rsin[:, ts(tt, 32)].rearrange("p (o k) -> p o k",
                                                       o=1)\
                        .broadcast_to([128, 4, 32])
                    sa = scr2[:, 0:256].rearrange("p (g k) -> p g k", k=64)
                    sb = scr2[:, 256:512].rearrange("p (g k) -> p g k", k=64)
                    e1, e2 = sa[:, :, 0:32], sa[:, :, 32:64]
                    o1, o2 = sb[:, :, 0:32], sb[:, :, 32:64]
                    v.tensor_mul(o1, pe, s1)
                    v.tensor_mul(o2, po, c1)
                    v.tensor_mul(e1, pe, c1)
                    ed(v, v.tensor_mul(e2, po, s1))
                    v.tensor_add(o1, o1, o2)
                    ed(v, v.tensor_sub(e1, e1, e2))
                    kb3 = kbf[:, 0:256].rearrange("p (g k) -> p g k", k=64)
                    r3 = smf[:, 36:40].rearrange("p (g o) -> p g o", o=1)\
                        .broadcast_to([128, 4, 32])
                    if k_pre_col is None:
                        v.tensor_mul(kb3[:, :, 0:32], e1, r3)
                        return v.tensor_mul(kb3[:, :, 32:64], o1, r3)
                    v.scalar_tensor_tensor(
                        kb3[:, :, 0:32], e1,
                        stats[:, 4 * T + 3:4 * T + 4], r3,
                        ALU.mult, ALU.mult)
                    return v.scalar_tensor_tensor(
                        kb3[:, :, 32:64], o1,
                        stats[:, 4 * T + 3:4 * T + 4], r3,
                        ALU.mult, ALU.mult)
                t_krope = prog.step("vector", krope,
                                    waits=[t_krs, t_ksq, g_kbf[0]] + t_rope,
                                    inc=sV)
                t_rope_rd[0] = t_krope
                g_k[0] = t_krope
                g_scr2q[0] = t_krope
                def ktp(t):
                    t.transpose(PS_TR[:, 0:128], kbf[:, 0:128], idn[:])
                    return t.transpose(PS_TR[:, 128:256], kbf[:, 128:256],
                                       idn[:])
                t_ktp = prog.step("tensor", ktp, waits=[t_krope, g_ptr[0]],
                                  inc=sP)
                g_kbf[0] = t_ktp
                def ktev(v, tt=tt):
                    last = None
                    for pair in range(2):
                        for half in range(2):
                            kv_i = pair * 2 + half
                            srcp = PS_TR[64 * half:64 * half + 64,
                                         pair * 128:(pair + 1) * 128]
                            dcol = B_KT + kv_i * 2048 + tt * 128
                            v.tensor_copy(arB[0:64, dcol:dcol + 128], srcp)
                            last = v.tensor_copy(arB[64:128, dcol:dcol + 128],
                                                 srcp)
                    return last
                t_ktev = prog.step("vector", ktev, waits=[t_ktp], inc=sV)
                g_ptr[0] = t_ktev
                toks.append(t_ktev)
                toks.append(t_vev)
                wprep_pump(2)

            for tt in range(NTS):
                uA(tt)
                if tt >= 1:
                    uB(tt - 1)
            uB(NTS - 1)
            return toks

        g_scr2q = [None]
        g_scr1g = [None]
        g_rsq = [None]
        g_kbf = [None]

        def emit_qbuild_units(sq_i, qb, is_glob, wset):
            slot = qb % 2
            rk_col = 2 if is_glob else 3
            units = []
            state = {}

            def u_tile(tq):
                tt = qb * 4 + tq
                T = sq_i * NTS + tt
                _, t_xnT = transpose_group(
                    lambda c, T=T: arA[:, T * 1024 + c * 128:
                                       T * 1024 + (c + 1) * 128],
                    lambda c: arB[:, B_XT + c * 128:B_XT + (c + 1) * 128],
                    8, waits=[tok_xn[T]])
                t_wbq = emit_wbuild(tt)
                def rsq(v, T=T, rk_col=rk_col):
                    return v.tensor_scalar(
                        smf[:, 52:53],
                        stats[:, 4 * T + rk_col:4 * T + rk_col + 1],
                        stats[:, 4 * T + rk_col:4 * T + rk_col + 1],
                        1.0 / HD, ALU.mult, ALU.mult)
                t_rsq = prog.step("vector", rsq,
                                  waits=[t_wbq, g_rsq[0]], inc=sV)
                state["xnT"] = t_xnT
                state["wbq"] = t_wbq
                state["rsq"] = t_rsq

            def u_half_a1(tq, hf):
                t_xnT = state["xnT"]
                def qproj(t, hf=hf):
                    last = None
                    for ic in range(8):
                        last = t.matmul(
                            PS_W[:, :],
                            arB[:, B_XT + ic * 128:B_XT + (ic + 1) * 128],
                            wqs[:, ic * 1024 + hf * 512:
                                ic * 1024 + hf * 512 + 512],
                            start=(ic == 0), stop=(ic == 7))
                    return last
                t_qp = prog.step("tensor", qproj,
                                 waits=[t_xnT, wset["wq"], g_psw[0]], inc=sP)
                t_wq_rd[0] = t_qp
                state[("qp", tq, hf)] = t_qp

            def u_half_a(tq, hf):
                tt = qb * 4 + tq
                T = sq_i * NTS + tt
                t_wbq = state["wbq"]
                t_qp = state.pop(("qp", tq, hf))
                def qsq(s):
                    return s.activation(scr2[:, 0:512], PS_W[:, :], AF.Square)
                t_qsq = prog.step("scalar", qsq,
                                  waits=[t_qp, g_scr2q[0]], inc=sA)
                def qwm(v):
                    s4 = scr2[:, 0:512].rearrange("p (g b k) -> p g b k",
                                                  g=8, b=2)
                    w4 = wsq[:, 0:32].rearrange("p (a b k) -> p a b k",
                                                a=1, b=1)\
                        .broadcast_to([128, 8, 2, 32])
                    ed(v, v.tensor_mul(s4, s4, w4))
                    return v.tensor_reduce(
                        smf[:, 0:8].rearrange("p (g o) -> p g o", o=1),
                        scr2[:, 0:512].rearrange("p (g k) -> p g k", k=64),
                        AX.X, ALU.add)
                t_qred = prog.step("vector", qwm, waits=[t_qsq, t_wbq], inc=sV)
                g_scr2q[0] = t_qred
                g_wsq[0] = t_qred
                state[("h", tq, hf)] = (t_qred, t_qsq)

            def u_half_b(tq, hf):
                tt = qb * 4 + tq
                T = sq_i * NTS + tt
                t_rsq = state["rsq"]
                (t_qred, t_qsq) = state.pop(("h", tq, hf))

                def qln(s):
                    return s.activation(smf[:, 8:16], smf[:, 0:8], AF.Ln,
                                        scale=smf[:, 52:53],
                                        bias=smf[:, 48:49])
                t_qln = prog.step("scalar", qln, waits=[t_qred, t_rsq],
                                  inc=sA)
                g_rsq[0] = t_qln
                def qexp(s):
                    return s.activation(smf[:, 8:16], smf[:, 8:16], AF.Exp,
                                        scale=-0.5)
                t_qrs = prog.step("scalar", qexp, waits=[t_qln], inc=sA)
                def qrope(v, tt=tt, hf=hf, T=T, rk_col=rk_col):
                    p3 = PS_W[:, :].rearrange("p (g k) -> p g k", k=64)
                    pe, po = p3[:, :, 0:32], p3[:, :, 32:64]
                    c1 = rcos[:, ts(tt, 32)]\
                        .rearrange("p (o k) -> p o k", o=1)\
                        .broadcast_to([128, 8, 32])
                    s1 = rsin[:, ts(tt, 32)]\
                        .rearrange("p (o k) -> p o k", o=1)\
                        .broadcast_to([128, 8, 32])
                    sa = scr1[:, 0:512].rearrange("p (g k) -> p g k", k=64)
                    sb = scr1[:, 512:1024].rearrange("p (g k) -> p g k", k=64)
                    e1, e2 = sa[:, :, 0:32], sa[:, :, 32:64]
                    o1, o2 = sb[:, :, 0:32], sb[:, :, 32:64]
                    v.tensor_mul(o1, pe, s1)
                    v.tensor_mul(o2, po, c1)
                    v.tensor_mul(e1, pe, c1)
                    ed(v, v.tensor_mul(e2, po, s1))
                    v.tensor_add(o1, o1, o2)
                    ed(v, v.tensor_sub(e1, e1, e2))
                    qb3 = arB[:, B_QB + hf * 512:B_QB + hf * 512 + 512]\
                        .rearrange("p (g k) -> p g k", k=64)
                    rq3 = smf[:, 8:16].rearrange("p (g o) -> p g o", o=1)\
                        .broadcast_to([128, 8, 32])
                    v.scalar_tensor_tensor(
                        qb3[:, :, 0:32], e1,
                        stats[:, 4 * T + rk_col:4 * T + rk_col + 1], rq3,
                        ALU.mult, ALU.mult)
                    return v.scalar_tensor_tensor(
                        qb3[:, :, 32:64], o1,
                        stats[:, 4 * T + rk_col:4 * T + rk_col + 1], rq3,
                        ALU.mult, ALU.mult)
                t_qrope = prog.step("vector", qrope,
                                    waits=[t_qrs, t_qsq] + t_rope, inc=sV)
                t_rope_rd[0] = t_qrope
                g_psw[0] = t_qrope
                g_scr1g[0] = t_qrope
                _, t_qtev = transpose_group(
                    lambda c, hf=hf: arB[:, B_QB + hf * 512 + c * 128:
                                         B_QB + hf * 512 + (c + 1) * 128],
                    lambda c, hf=hf, slot=slot, tq=tq: arB[
                        :, B_QT + slot * 4096 + (hf * 4 + c) * 512 +
                        tq * 128:
                        B_QT + slot * 4096 + (hf * 4 + c) * 512 +
                        tq * 128 + 128],
                    4, waits=[t_qrope, g_qt[slot]])
                qb_ready[slot] = t_qtev
                wprep_pump(1)

            for tq in range(4):
                units.append(lambda tq=tq: u_tile(tq))
                for hf in range(2):
                    units.append(lambda tq=tq, hf=hf: u_half_a1(tq, hf))
                    units.append(lambda tq=tq, hf=hf: u_half_a(tq, hf))
                    units.append(lambda tq=tq, hf=hf: u_half_b(tq, hf))
            return units

        qb_ready = [None, None]

        def emit_attn(sq_i, is_glob, wset, t_kv_bar):
            S_T = [PS_S0, PS_S1]
            for u in emit_qbuild_units(sq_i, 0, is_glob, wset):
                u()
            for qb in range(4):
                q0 = qb * 512
                t_qt = qb_ready[qb % 2]
                pending = (emit_qbuild_units(sq_i, qb + 1, is_glob, wset)
                           if qb < 3 else [])
                pend_i = [0]
                def pump_q(n=2):
                    for _ in range(n):
                        if pend_i[0] < len(pending):
                            pending[pend_i[0]]()
                            pend_i[0] += 1
                ep_last = None
                last_qk = None
                if is_glob:
                    kts = list(range(0, 4 * qb + 4))
                else:
                    kts = [k for k in range(4 * qb - 2, 4 * qb + 4) if k >= 0]
                n_kt = len(kts)
                for hp in range(HP):
                    kv_i = hp // 2
                    qk_toks = {}
                    exp_toks = {}
                    geom = []
                    for kt in kts:
                        k0 = kt * 128
                        q_s = max(q0, k0)
                        q_e = q0 + 512 if is_glob else min(k0 + 384, q0 + 512)
                        geom.append((kt, k0, q_s, q_e - q_s, q_s - q0))
                    def emit_qk(i):
                        kt, k0, q_s, w, q_off = geom[i]
                        r = i % 2
                        def qk(t, hp=hp, kt=kt, kv_i=kv_i, r=r, q_off=q_off,
                               w=w, slot=qb % 2):
                            S = S_T[r]
                            lastm = None
                            for hh in range(2):
                                b = hh * 64
                                lastm = t.matmul(
                                    S[:, hh * 512:hh * 512 + w],
                                    arB[b:b + 64,
                                        B_KT + kv_i * 2048 + kt * 128:
                                        B_KT + kv_i * 2048 + (kt + 1) * 128],
                                    arB[b:b + 64,
                                        B_QT + slot * 4096 + hp * 512 + q_off:
                                        B_QT + slot * 4096 + hp * 512 +
                                        q_off + w],
                                    start=True, stop=True)
                            return lastm
                        qk_toks[i] = prog.step(
                            "tensor", qk, waits=[t_qt, t_kv_bar, g_pss[r]],
                            inc=sP)
                    emit_qk(0)
                    for i in range(n_kt):
                        kt, k0, q_s, w, q_off = geom[i]
                        r = i % 2
                        er = i % 2
                        def expf(s, r=r, er=er, w=w):
                            return s.activation(
                                arB[:, B_EB + er * 1024:
                                    B_EB + er * 1024 + 512 + w],
                                S_T[r][:, 0:512 + w], AF.Exp, scale=0.125)
                        t_exp = prog.step("scalar", expf,
                                          waits=[qk_toks[i], g_eb[er]],
                                          inc=sA)
                        g_pss[r] = t_exp
                        exp_toks[i] = t_exp
                        if i + 1 < n_kt:
                            emit_qk(i + 1)
                        base = q_s - k0
                        need_c = base < 128
                        need_u = (not is_glob) and (base + w - 1 >= 256)
                        t_msk = None
                        if need_c or need_u:
                            def msk(g, er=er, w=w, base=base, need_c=need_c,
                                    need_u=need_u):
                                last = None
                                for hh in range(2):
                                    off = B_EB + er * 1024 + hh * 512
                                    if need_c:
                                        wd = min(w, 128 - base)
                                        ap = arB[:, off:off + wd]
                                        last = g.affine_select(
                                            out=ap, in_=ap,
                                            compare_op=ALU.is_ge, fill=0.0,
                                            base=base, pattern=[[1, wd]],
                                            channel_multiplier=-1)
                                        if need_u:
                                            ed(g, last, "gpsimd")
                                    if need_u:
                                        lo = max(0, 256 - base - 127)
                                        ap = arB[:, off + lo:off + w]
                                        last = g.affine_select(
                                            out=ap, in_=ap,
                                            compare_op=ALU.is_ge, fill=0.0,
                                            base=255 - base - lo,
                                            pattern=[[-1, w - lo]],
                                            channel_multiplier=1)
                                return last
                            t_msk = prog.step("gpsimd", msk, waits=[t_exp],
                                              inc=sM)
                        if i == 0 and not is_glob:
                            def pvz(t):
                                t.matmul(PS_O0[:, :], zcol[0:1, :],
                                         arB[0:1, B_V:B_V + 512], start=True,
                                         stop=False)
                                return t.matmul(PS_O1[:, :], zcol[0:1, :],
                                                arB[0:1, B_V:B_V + 512],
                                                start=True, stop=False)
                            prog.step("tensor", pvz,
                                      waits=[g_pso[0], g_pso[1], t_kv_bar],
                                      inc=sP)
                        def pv(t, kt=kt, kv_i=kv_i, er=er, w=w, q_off=q_off,
                               first=(i == 0 and is_glob),
                               last=(i == n_kt - 1)):
                            lastm = None
                            for hh, PO in ((0, PS_O0), (1, PS_O1)):
                                lastm = t.matmul(
                                    PO[:, q_off:q_off + w],
                                    arB[:, B_V + kt * 512 + kv_i * 128:
                                        B_V + kt * 512 + (kv_i + 1) * 128],
                                    arB[:, B_EB + er * 1024 + hh * 512:
                                        B_EB + er * 1024 + hh * 512 + w],
                                    start=first, stop=last)
                            return lastm
                        w8 = [t_exp, t_msk]
                        if i == 0 and is_glob:
                            w8 += [g_pso[0], g_pso[1]]
                        t_pv = prog.step("tensor", pv, waits=w8, inc=sP)
                        g_eb[er] = t_pv
                        if i == n_kt // 3 or (hp >= 4 and
                                              i == (2 * n_kt) // 3):
                            pump_q(1)
                    last_qk = qk_toks[n_kt - 1]
                    def epi(v, hp=hp):
                        v.reciprocal(rz[:, :], PS_O0[64:128, 0:512])
                        ed(v, v.reciprocal(rz2[:, :], PS_O1[64:128, 0:512]))
                        v.tensor_mul(
                            arB[0:64, B_AO + hp * 512:B_AO + (hp + 1) * 512],
                            PS_O0[0:64, :], rz[:, :])
                        return v.tensor_mul(
                            arB[64:128, B_AO + hp * 512:B_AO + (hp + 1) * 512],
                            PS_O1[0:64, :], rz2[:, :])
                    t_epi = prog.step("vector", epi, waits=[t_pv, g_ao[0]],
                                      inc=sV)
                    g_pso[0] = t_epi
                    g_pso[1] = t_epi
                    ep_last = t_epi
                    pump_q(1)
                    wprep_pump(1)
                g_qt[qb % 2] = last_qk
                wo_gate = [g_pso[0], g_pso[1]]
                widx = 0
                last_wo_pe = None
                for tq in range(4):
                    T = sq_i * NTS + qb * 4 + tq
                    for ob in range(2):
                        PSx = PS_O0 if widx % 2 == 0 else PS_O1
                        def wo(t, tq=tq, ob=ob, PSx=PSx):
                            last = None
                            for ic in range(8):
                                last = t.matmul(
                                    PSx[:, :],
                                    arB[:, B_AO + ic * 512 + tq * 128:
                                        B_AO + ic * 512 + tq * 128 + 128],
                                    wos[:, ic * 1024 + ob * 512:
                                        ic * 1024 + ob * 512 + 512],
                                    start=(ic == 0), stop=(ic == 7))
                            return last
                        t_wo = prog.step("tensor", wo,
                                         waits=[ep_last, wo_gate[widx % 2],
                                                wset["wo"]], inc=sP)
                        t_wo_rd[0] = t_wo
                        last_wo_pe = t_wo
                        if is_glob:
                            def woev(v, T=T, ob=ob, PSx=PSx):
                                dst = arA[:, T * 1024 + ob * 512:
                                          T * 1024 + ob * 512 + 512]
                                return v.tensor_add(dst, PSx[:, :], dst)
                            t_ev = prog.step("vector", woev, waits=[t_wo],
                                             inc=sV)
                        else:
                            def woev(v, T=T, ob=ob, PSx=PSx):
                                src = arA[:, T * 1024 + ob * 512:
                                          T * 1024 + ob * 512 + 512]
                                dsts = ssb[:, (T % 2) * 1024 + ob * 512:
                                           (T % 2) * 1024 + ob * 512 + 512]
                                return v.tensor_add(dsts, PSx[:, :], src)
                            t_ev = prog.step("vector", woev,
                                             waits=[t_wo, g_ssb[T % 2]],
                                             inc=sV)
                        wo_gate[widx % 2] = t_ev
                        if widx % 2 == 0:
                            g_pso[0] = t_ev
                        else:
                            g_pso[1] = t_ev
                        widx += 1
                        pump_q(1)
                        if is_glob:
                            tok_xn[T] = t_ev
                        elif ob == 1:
                            tst = dma("sync", s_scr[ts(T, 128), :],
                                      ssb[:, (T % 2) * 1024:
                                          (T % 2) * 1024 + 1024],
                                      waits=[t_ev])
                            g_ssb[T % 2] = tst
                            if "s" in dbg:
                                g_ssb[T % 2] = dma(
                                    "sync", dbg["s"][ts(T, 128), :],
                                    ssb[:, (T % 2) * 1024:(T % 2) * 1024 + 1024],
                                    waits=[t_ev, tst])
                g_ao[0] = last_wo_pe
                for u in pending[pend_i[0]:]:
                    u()

        wset_g = load_glob_weights() if "p1" in phases else None
        p5_w1_gates = [None, None]
        if "p1" in phases:
            for sq_i in range(NSEQ):
                if sq_i > 0:
                    wset_g = load_glob_weights(
                        waits=[t_wq_rd[0], t_wkv_rd[0], t_wo_rd[0]])
                    t_rope = [dma("sync", rcos[:, :], cosg[:, :],
                                  waits=[t_rope_rd[0]]),
                              dma("sync", rsin[:, :], sing[:, :])]
                kv_toks = emit_kv_build(sq_i, True, wset_g)
                t_bar = prog.barrier("vector", max_toks(kv_toks) + t_rope, sV)
                if "p2" in phases:
                    emit_attn(sq_i, True, wset_g, t_bar)
                    if "x1" in dbg and sq_i == NSEQ - 1:
                        for T in range(NT):
                            dma("sync", dbg["x1"][ts(T, 128), :],
                                arA[:, ts(T, 1024)], waits=[tok_xn[T]])
                if "p3" in phases:
                    wset_l = load_loc_weights(
                        waits=[t_wq_rd[0], t_wkv_rd[0], t_wo_rd[0]])
                    t_rope = [dma("sync", rcos[:, :], cosl[:, :],
                                  waits=[t_rope_rd[0]]),
                              dma("sync", rsin[:, :], sinl[:, :])]
                    kv_toks = emit_kv_build(sq_i, False, wset_l)
                    t_bar2 = prog.barrier("vector",
                                          max_toks(kv_toks) + t_rope, sV)
                    emit_attn(sq_i, False, wset_l, t_bar2)
                if sq_i == 0:
                    p5_w1_gates[0] = max_toks(
                        [t_wq_rd[0], t_wo_rd[0], t_wkv_rd[0],
                         tok_xn[NTS - 1]])
            if "kt" in dbg:
                dma("sync", dbg["kt"][:, :], arB[:, B_KT:B_KT + 8192],
                    waits=[(sV, sV.n)])
            if "vt" in dbg:
                dma("sync", dbg["vt"][:, :], arB[:, B_V:B_V + 8192],
                    waits=[(sV, sV.n), (sA, sA.n)])
            if "qt" in dbg:
                dma("sync", dbg["qt"][:, :], arB[:, B_QT:B_QT + 4096],
                    waits=[(sV, sV.n)])
        if "p5" in phases:
            t_allstore = wprep_finish()
            bar_attn = prog.barrier("vector",
                                    [(sV, sV.n), (sA, sA.n), (sP, sP.n),
                                     (sM, sM.n)], sV)
            t_w1 = []
            for ic in range(8):
                t_w1.append(dma("sync", arA[:, ts(ic, 4096)],
                                w1_scr[ts(ic, 128), :],
                                waits=t_allstore + [bar_attn]))
            t_w2 = []
            for fc in range(32):
                t_w2.append(dma("sync", arB[:, ts(fc, 1024)],
                                w2_scr[ts(fc, 128), :],
                                waits=t_allstore + [bar_attn]))
            bar_bpr = None
            for hb in range(2):
                t_row = dma("sync", scr2[0:1, 0:512],
                            bprR[0:1, ts(hb, 512)],
                            waits=[bar_attn, bar_bpr])
                def bpr_bc(t):
                    return t.matmul(PS_W[:, :], onesf[0:1, :],
                                    scr2[0:1, 0:512], start=True, stop=True)
                tp0 = prog.step("tensor", bpr_bc, waits=[t_row, g_psw[0]],
                                inc=sP)
                def bpr_cp(v, hb=hb):
                    return v.tensor_copy(bprb[:, ts(hb, 512)], PS_W[:, :])
                bar_bpr = prog.step("vector", bpr_cp, waits=[tp0], inc=sV)
                g_psw[0] = bar_bpr

            g_bls = [t_gpset]
            fold1 = []
            t_afc_prev = [None]
            fold_ps = [PS_W[:, :], PS_S0[:, 0:512], PS_S0[:, 512:1024],
                       PS_S1[:, 0:512], PS_S1[:, 512:1024]]
            g_fps = [g_psw[0], g_pss[0], g_pss[0], g_pss[1], g_pss[1]]
            fidx = [0]
            for fb in range(8):
                tb_ = dma("gpsimd", bls[:, :], bfcL[:, ts(fb, 512)],
                          waits=[g_bls[0]])
                for ic in range(8):
                    ta_f = dma("gpsimd", afcs[:, :], afc[:, ts(ic, 128)],
                               waits=[t_afc_prev[0]])
                    j = fidx[0] % 5
                    PSf = fold_ps[j]
                    def lmm(t, PSf=PSf):
                        return t.matmul(PSf, afcs[:, :],
                                        bls[:, :], start=True, stop=True)
                    t_lm = prog.step("tensor", lmm,
                                     waits=[tb_, ta_f, g_fps[j], t_w1[ic]],
                                     inc=sP)
                    t_afc_prev[0] = t_lm
                    def ladd(v, ic=ic, fb=fb, PSf=PSf):
                        dst = arA[:, ic * 4096 + fb * 512:
                                  ic * 4096 + fb * 512 + 512]
                        return v.scalar_tensor_tensor(dst, PSf, 2.0,
                                                      dst, ALU.mult, ALU.add)
                    t_la = prog.step("vector", ladd, waits=[t_lm], inc=sV)
                    g_fps[j] = t_la
                    fidx[0] += 1
                    fold1.append(t_la)
                g_bls[0] = fold1[-1]
            fold2 = []
            t_apr_prev = [None]
            for db in range(2):
                tb_ = dma("gpsimd", bls[:, 0:512], bprL[:, ts(db, 512)],
                          waits=[g_bls[0]])
                for fc in range(32):
                    ta_ = dma("gpsimd", aprs[:, :], apr[:, ts(fc, 128)],
                              waits=[t_apr_prev[0]])
                    j = fidx[0] % 5
                    PSf = fold_ps[j]
                    def lmm(t, PSf=PSf):
                        return t.matmul(PSf, aprs[:, :], bls[:, 0:512],
                                        start=True, stop=True)
                    t_lm = prog.step("tensor", lmm,
                                     waits=[tb_, ta_, g_fps[j], t_w2[fc]],
                                     inc=sP)
                    t_apr_prev[0] = t_lm
                    def ladd(v, fc=fc, db=db, PSf=PSf):
                        dst = arB[:, fc * 1024 + db * 512:
                                  fc * 1024 + db * 512 + 512]
                        return v.scalar_tensor_tensor(dst, PSf, 2.0,
                                                      dst, ALU.mult, ALU.add)
                    t_la = prog.step("vector", ladd, waits=[t_lm], inc=sV)
                    g_fps[j] = t_la
                    fidx[0] += 1
                    fold2.append(t_la)
                g_bls[0] = fold2[-1]
            bar_w1 = prog.barrier("vector", max_toks(fold1), sV)
            bar_w2 = prog.barrier("vector", max_toks(fold2), sV)
            g_psw[0] = bar_w2
            g_pss[0] = bar_w2
            g_pss[1] = bar_w2
            g_pso[0] = bar_w2
            g_pso[1] = bar_w2
            if "w1" in dbg:
                for ic in range(8):
                    dma("sync", dbg["w1"][ts(ic, 128), :], arA[:, ts(ic, 4096)],
                        waits=[bar_w1])
            if "w2" in dbg:
                for fc in range(32):
                    dma("sync", dbg["w2"][ts(fc, 128), :], arB[:, ts(fc, 1024)],
                        waits=[bar_w2])

            W_XT, W_XN = 0, 2048
            g_fc = [None, None, None, None]
            g_hT = [None]
            g_out = [None, None]
            g_z = [None, None]
            g_g = [None]
            pj_ps = [PS_O0, PS_O1, PS_W]
            g_pj = [None, None, g_psw[0]]
            for tb in range(int(_os.environ.get("P5NB", 16))):
                tiles = [tb * 2, tb * 2 + 1]
                t_xnT_all = []
                t_x_ld = {}
                t_s_ld = {}
                for i, T in enumerate(tiles):
                    sq_i, t_i = divmod(T, NTS)
                    slot = T % 2
                    tld = dma("sync", xi[:, ts(slot, 1024)],
                              x_in[sq_i, ts(t_i, 128), :],
                              waits=[g_out[slot], g_xi[slot]])
                    t_x_ld[T] = tld
                    t_s_ld[T] = dma("sync", ssb[:, ts(slot, 1024)],
                                    s_scr[ts(T, 128), :],
                                    waits=[g_ssb[slot], bar_attn])
                    g_ssb[slot] = None
                    def lnr(v, T=T, slot=slot):
                        return v.tensor_scalar(
                            wos[:, W_XN:W_XN + 1024], xi[:, ts(slot, 1024)],
                            stats[:, 4 * T:4 * T + 1],
                            stats[:, 4 * T + 1:4 * T + 2],
                            ALU.subtract, ALU.mult)
                    t_ln = prog.step("vector", lnr,
                                     waits=[tld, bar_w2, g_hT[0],
                                            (t_xnT_all[-1] if t_xnT_all
                                             else None)], inc=sV)
                    g_xi[slot] = t_ln
                    _, t_tx = transpose_group(
                        lambda c: wos[:, W_XN + c * 128:W_XN + (c + 1) * 128],
                        lambda c, i=i: wos[:, W_XT + c * 256 + i * 128:
                                           W_XT + c * 256 + i * 128 + 128],
                        8, waits=[t_ln])
                    t_xnT_all.append(t_tx)
                bar_xnT = prog.barrier("vector", max_toks(t_xnT_all), sV)
                t_gel_last = None
                for f8 in range(32):
                    Sx = PS_S0 if (f8 % 4) < 2 else PS_S1
                    So = (f8 % 2) * 512
                    def fcm(t, f8=f8, Sx=Sx, So=So):
                        last = None
                        for ic in range(8):
                            last = t.matmul(
                                Sx[:, So:So + 256],
                                arA[:, ic * 4096 + f8 * 128:
                                    ic * 4096 + f8 * 128 + 128],
                                wos[:, W_XT + ic * 256:W_XT + ic * 256 + 256],
                                start=(ic == 0), stop=(ic == 7))
                        return last
                    t_fc = prog.step("tensor", fcm,
                                     waits=[bar_xnT, bar_w1, g_fc[f8 % 4]],
                                     inc=sP)
                    def gel(s, f8=f8, Sx=Sx, So=So):
                        return s.activation(
                            wqs[:, ts(f8, 256)], Sx[:, So:So + 256],
                            AF.Gelu, bias=bfcs[:, f8:f8 + 1])
                    t_g = prog.step("scalar", gel, waits=[t_fc, g_hT[0]],
                                    inc=sA)
                    g_fc[f8 % 4] = t_g
                    t_gel_last = t_g
                if tb == 0 and "h0" in dbg:
                    dma("sync", dbg["h0"][:, :], wqs[:, :],
                        waits=[t_gel_last])
                pj_idx = 0
                last_pj = None
                for qt in range(2):
                    T = tiles[qt]
                    slot = T % 2
                    for ob in range(2):
                        psl = pj_idx % 3
                        POap = pj_ps[psl][:, :]
                        def pjm(t, qt=qt, ob=ob, POap=POap):
                            last = None
                            for f8 in range(32):
                                last = t.matmul(
                                    POap,
                                    wqs[:, f8 * 256 + qt * 128:
                                        f8 * 256 + qt * 128 + 128],
                                    arB[:, f8 * 1024 + ob * 512:
                                        f8 * 1024 + ob * 512 + 512],
                                    start=(f8 == 0), stop=(f8 == 31))
                            return last
                        t_pj = prog.step("tensor", pjm,
                                         waits=[t_gel_last, g_pj[psl],
                                                bar_w2], inc=sP)
                        last_pj = t_pj
                        def zadd(v, POap=POap, ob=ob, qt=qt):
                            dst = scr1[:, ts(qt, 512)]
                            return v.tensor_tensor(dst, POap,
                                                   bprb[:, ts(ob, 512)],
                                                   ALU.add)
                        t_z = prog.step("vector", zadd,
                                        waits=[t_pj, bar_bpr, g_z[qt]],
                                        inc=sV)
                        g_pj[psl] = t_z
                        def sil(s, qt=qt):
                            return s.activation(scr2[:, :],
                                                scr1[:, ts(qt, 512)],
                                                AF.Silu)
                        t_si = prog.step("scalar", sil,
                                         waits=[t_z, g_g[0]], inc=sA)
                        g_z[qt] = t_si
                        def comb(v, slot=slot, ob=ob):
                            ed(v, v.tensor_mul(
                                scr2[:, :], scr2[:, :],
                                ssb[:, slot * 1024 + ob * 512:
                                    slot * 1024 + ob * 512 + 512]))
                            dst = xi[:, slot * 1024 + ob * 512:
                                     slot * 1024 + ob * 512 + 512]
                            return v.tensor_add(dst, scr2[:, :], dst)
                        t_cb = prog.step("vector", comb,
                                         waits=[t_si, t_s_ld[T], t_x_ld[T]],
                                         inc=sV)
                        g_g[0] = t_cb
                        if ob == 1:
                            sq_i, t_i = divmod(T, NTS)
                            t_out = dma("sync", out_t[sq_i, ts(t_i, 128), :],
                                        xi[:, ts(slot, 1024)], waits=[t_cb])
                            g_out[slot] = t_out
                            g_xi[slot] = t_cb
                            g_ssb[slot] = t_cb
                        pj_idx += 1
                g_hT[0] = last_pj

        fin = [(L, L.n) for L in lanes["sync"] if L.n > 0] + \
              [(L, L.n) for L in lanes["gpsimd"] if L.n > 0]
        def final_wait(e, _fin=fin):
            last = None
            for L, v in _fin:
                last = e.wait_ge(L.h, v)
            return last
        prog.steps["sync"].append((final_wait, [], None))

        prog.check()
        block = E(nc.Block())
        prog.emit(block)

    return nc, dbg


def _perm_deinterleave(nheads, hd=HD):
    p = []
    for h in range(nheads):
        base = h * hd
        p += [base + 2 * j for j in range(hd // 2)]
        p += [base + 2 * j + 1 for j in range(hd // 2)]
    return np.asarray(p)


def _rope_tables(base):
    import jax
    import jax.numpy as jnp
    with jax.default_device(jax.devices("cpu")[0]):
        inv = 1.0 / base ** (jnp.arange(0, HD, 2, dtype=jnp.float32) / HD)
        fr = jnp.arange(SEQ, dtype=jnp.float32)[:, None] * inv[None, :]
        cos = np.asarray(jnp.cos(fr), dtype=np.float32)
        sin = np.asarray(jnp.sin(fr), dtype=np.float32)
    cos = cos.reshape(NTS, 128, 32).transpose(1, 0, 2).reshape(128, 512)
    sin = sin.reshape(NTS, 128, 32).transpose(1, 0, 2).reshape(128, 512)
    return np.ascontiguousarray(cos), np.ascontiguousarray(sin)


_BUILD_CACHE = {}
TAP_RESULTS = {}


def _get_build(debug=(), phases=("p0", "p1", "p2", "p3", "p5")):
    key = (tuple(sorted(debug)), tuple(phases))
    if key not in _BUILD_CACHE:
        _BUILD_CACHE[key] = build(debug=debug, phases=phases)
    return _BUILD_CACHE[key]


def prep_inputs(inputs):
    f = np.ascontiguousarray
    pq = _perm_deinterleave(H)
    pk = _perm_deinterleave(KVH)
    cg, sg = _rope_tables(1000000.0)
    cl, sl = _rope_tables(10000.0)
    common = {
        "wq_g": f(inputs["L0_attn_g_wq"][:, pq]),
        "kvd": f(inputs["L0_attn_g_kv_down"]),
        "kup": f(inputs["L0_attn_g_k_up"][:, pk]),
        "vup": f(inputs["L0_attn_g_v_up"]),
        "gwo": f(inputs["L0_attn_g_wo"]),
        "lwq": f(inputs["L0_attn_l_wq"][:, pq]),
        "lwk": f(inputs["L0_attn_l_wk"][:, pk]),
        "lwv": f(inputs["L0_attn_l_wv"]),
        "lwo": f(inputs["L0_attn_l_wo"]),
        "nw1T": f(inputs["L0_attn_norm1_w"].reshape(8, 128).T),
        "nw2T": f(inputs["L0_attn_norm2_w"].reshape(8, 128).T),
        "wfcT": f(inputs["L0_ffn_W_fc"].T),
        "wprT": f(inputs["L0_ffn_W_proj"].T),
        "bfcT": f(inputs["L0_ffn_b_fc"].reshape(32, 128).T),
        "bprR": f(inputs["L0_ffn_b_proj"].reshape(1, D)),
        "afc": f(inputs["L0_ffn_A_fc"]),
        "bfcL": f(inputs["L0_ffn_B_fc"].T),
        "apr": f(inputs["L0_ffn_A_proj"]),
        "bprL": f(inputs["L0_ffn_B_proj"].T),
        "cosg": cg, "sing": sg, "cosl": cl, "sinl": sl,
    }
    x = np.asarray(inputs["L0_attn_hidden"])
    in_maps = []
    for c in range(8):
        m = dict(common)
        m["x"] = f(x[2 * c:2 * c + 2])
        in_maps.append(m)
    return in_maps


def kernel(**inputs):
    import os as _os
    taps = tuple(t for t in _os.environ.get("KTAPS", "").split(",") if t)
    nc, _ = _get_build(debug=taps)
    in_maps = prep_inputs(inputs)
    ncores = int(_os.environ.get("KCORES", "8"))
    res = run_bass_kernel_spmd(nc, in_maps[:ncores], core_ids=list(range(ncores)))
    if taps:
        TAP_RESULTS.clear()
        for k in taps:
            TAP_RESULTS[k] = np.stack(
                [np.asarray(r["dbg_" + k]) for r in res.results])
    out = np.concatenate([np.asarray(r["out"]) for r in res.results], axis=0)
    return out.astype(np.float32)


# version 16 (final)
# speedup vs baseline: 1.0143x; 1.0143x over previous
import math
from contextlib import ExitStack

import numpy as np
import ml_dtypes

import concourse.bass as bass
import concourse.mybir as mybir
from concourse.bass_utils import run_bass_kernel_spmd

F32 = mybir.dt.float32
BF16 = mybir.dt.bfloat16
FP16 = mybir.dt.float16
I32 = mybir.dt.int32
AF = mybir.ActivationFunctionType
ALU = mybir.AluOpType
AX = mybir.AxisListType

D = 1024
SEQ = 2048
NTS = 16
NSEQ = 2
NT = NTS * NSEQ
H = 16
KVH = 4
HD = 64
HP = 8
LAT = 256
LN_EPS = 1e-5
QK_EPS = 1e-6

ALL_ENGINES = ("sync", "vector", "scalar", "tensor", "gpsimd")


class Sem:
    def __init__(self, handle, name, engine=None, is_dma=False):
        self.h = handle
        self.name = name
        self.n = 0
        self.engine = engine
        self.is_dma = is_dma

    def tick(self, k=1):
        self.n += k
        return (self, self.n)


class Prog:
    def __init__(self):
        self.steps = {e: [] for e in ALL_ENGINES}

    def step(self, eng, fn, waits=(), inc=None):
        tok = None
        inc_t = None
        if inc is not None:
            sem, amt = inc if isinstance(inc, tuple) else (inc, 1)
            tok = sem.tick(amt)
            inc_t = (sem, amt)
        ws = [w for w in waits if w is not None]
        self.steps[eng].append((fn, ws, inc_t))
        return tok

    def barrier(self, eng, waits, inc):
        return self.step(eng, lambda e: e.engine_nop(), waits=waits, inc=inc)

    def emit(self, block):
        def make(eng_name):
            def body(e):
                seen = {}
                for fn, waits, inc_t in self.steps[eng_name]:
                    for sem, val in waits:
                        if (sem.engine == eng_name and not sem.is_dma
                                and eng_name == "tensor"):
                            continue
                        if seen.get(sem, -1) >= val:
                            continue
                        e.wait_ge(sem.h, val)
                        seen[sem] = val
                    ins = fn(e)
                    if inc_t is not None:
                        sem, amt = inc_t
                        ins.then_inc(sem.h, amt)
            return body
        block.sync(make("sync"))
        block.vector(make("vector"))
        block.scalar(make("scalar"))
        block.tensor(make("tensor"))
        block.gpsimd(make("gpsimd"))

    def check(self):
        counts = {}
        idx = {e: 0 for e in ALL_ENGINES}
        total = sum(len(v) for v in self.steps.values())
        done = 0
        progress = True
        while progress:
            progress = False
            for e in ALL_ENGINES:
                while idx[e] < len(self.steps[e]):
                    fn, waits, inc_t = self.steps[e][idx[e]]
                    if any(counts.get(s, 0) < v for s, v in waits
                           if not (s.engine == e and not s.is_dma
                                   and e == "tensor")):
                        break
                    if inc_t is not None:
                        s, amt = inc_t
                        counts[s] = counts.get(s, 0) + amt
                    idx[e] += 1
                    done += 1
                    progress = True
        if done != total:
            msgs = []
            for e in ALL_ENGINES:
                if idx[e] < len(self.steps[e]):
                    fn, waits, inc_t = self.steps[e][idx[e]]
                    bad = [(s.name, v, counts.get(s, 0)) for s, v in waits
                           if counts.get(s, 0) < v]
                    msgs.append(f"{e}@{idx[e]}/{len(self.steps[e])}: {bad}")
            raise RuntimeError("DEADLOCK " + " | ".join(msgs))


def ts(i, n):
    return slice(i * n, (i + 1) * n)


def max_toks(toks):
    best = {}
    for t in toks:
        if t is None:
            continue
        s, v = t
        if s not in best or v > best[s]:
            best[s] = v
    return [(s, v) for s, v in best.items()]


def build(debug=(), phases=("p0", "p1", "p2", "p3", "p5")):
    nc = bass.Bass("TRN2", target_bir_lowering=False)
    dt = nc.dram_tensor

    x_in = dt("x", [NSEQ, SEQ, D], F32, kind="ExternalInput").ap()
    out_t = dt("out", [NSEQ, SEQ, D], F32, kind="ExternalOutput").ap()
    wq_g = dt("wq_g", [D, D], F32, kind="ExternalInput").ap()
    kvd = dt("kvd", [D, LAT], F32, kind="ExternalInput").ap()
    kup = dt("kup", [LAT, LAT], F32, kind="ExternalInput").ap()
    vup = dt("vup", [LAT, LAT], F32, kind="ExternalInput").ap()
    gwo = dt("gwo", [D, D], F32, kind="ExternalInput").ap()
    lwq = dt("lwq", [D, D], F32, kind="ExternalInput").ap()
    lwk = dt("lwk", [D, LAT], F32, kind="ExternalInput").ap()
    lwv = dt("lwv", [D, LAT], F32, kind="ExternalInput").ap()
    lwo = dt("lwo", [D, D], F32, kind="ExternalInput").ap()
    nw1T = dt("nw1T", [128, 8], F32, kind="ExternalInput").ap()
    nw2T = dt("nw2T", [128, 8], F32, kind="ExternalInput").ap()
    wfcT = dt("wfcT", [D, 4096], F32, kind="ExternalInput").ap()
    wprT = dt("wprT", [4096, D], F32, kind="ExternalInput").ap()
    bfcT = dt("bfcT", [128, 32], F32, kind="ExternalInput").ap()
    bprR = dt("bprR", [1, D], F32, kind="ExternalInput").ap()
    afc = dt("afc", [16, D], F32, kind="ExternalInput").ap()
    bfcL = dt("bfcL", [16, 4096], F32, kind="ExternalInput").ap()
    apr = dt("apr", [16, 4096], F32, kind="ExternalInput").ap()
    bprL = dt("bprL", [16, D], F32, kind="ExternalInput").ap()
    cosg = dt("cosg", [128, 512], F32, kind="ExternalInput").ap()
    sing = dt("sing", [128, 512], F32, kind="ExternalInput").ap()
    cosl = dt("cosl", [128, 512], F32, kind="ExternalInput").ap()
    sinl = dt("sinl", [128, 512], F32, kind="ExternalInput").ap()

    s_scr = dt("s_scr", [NT * 128, D], FP16, kind="Internal").ap()
    xp_scr = dt("xp_scr", [2, 128], F32, kind="Internal").ap()
    w1_scr = dt("w1_scr", [D, 4096], BF16, kind="Internal").ap()
    w2_scr = dt("w2_scr", [4096, D], BF16, kind="Internal").ap()

    dbg = {}
    def dbg_t(name, shape, dtype=F32):
        if name in debug:
            dbg[name] = dt("dbg_" + name, shape, dtype, kind="ExternalOutput").ap()
        return dbg.get(name)

    dbg_t("xn", [NT * 128, D], BF16)
    dbg_t("x1", [NT * 128, D], BF16)
    dbg_t("s", [NT * 128, D], FP16)
    dbg_t("kt", [128, 8192], BF16)
    dbg_t("vt", [128, 8192], BF16)
    dbg_t("qt", [128, 4096], BF16)
    dbg_t("w1", [D, 4096], BF16)
    dbg_t("w2", [4096, D], BF16)
    dbg_t("h0", [128, 8192], BF16)
    dbg_t("stats", [128, 128], F32)
    dbg_t("smf", [128, 64], F32)

    with ExitStack() as ctx:
        E = ctx.enter_context

        arA = E(nc.sbuf_tensor("arA", [128, 32768], BF16))
        arB = E(nc.sbuf_tensor("arB", [128, 32768], BF16))
        wqs = E(nc.sbuf_tensor("wqs", [128, 8192], BF16))
        wos = E(nc.sbuf_tensor("wos", [128, 8192], BF16))
        wkv = E(nc.sbuf_tensor("wkv", [128, 4096], BF16))
        scr1 = E(nc.sbuf_tensor("scr1", [128, 1024], F32))
        scr2 = E(nc.sbuf_tensor("scr2", [128, 512], F32))
        xi = E(nc.sbuf_tensor("xi", [128, 2048], F32))
        ssb = E(nc.sbuf_tensor("ssb", [128, 2048], FP16))
        wp_i = E(nc.sbuf_tensor("wp_i", [128, 512], I32))
        wp_c = E(nc.sbuf_tensor("wp_c", [128, 512], F32))
        wp_q = E(nc.sbuf_tensor("wp_q", [128, 512], BF16))
        rz = E(nc.sbuf_tensor("rz", [64, 512], F32))
        rz2 = E(nc.sbuf_tensor("rz2", [64, 512], F32))
        rcos = E(nc.sbuf_tensor("rcos", [128, 512], F32))
        rsin = E(nc.sbuf_tensor("rsin", [128, 512], F32))
        stats = E(nc.sbuf_tensor("stats", [128, 128], F32))
        bfcs = E(nc.sbuf_tensor("bfcs", [128, 32], F32))
        bprb = E(nc.sbuf_tensor("bprb", [128, 1024], FP16))
        nws = E(nc.sbuf_tensor("nws", [128, 16], F32))
        kbf = E(nc.sbuf_tensor("kbf", [128, 256], BF16))
        latb = E(nc.sbuf_tensor("latb", [128, 256], BF16))
        smf = E(nc.sbuf_tensor("smf", [128, 64], F32))
        onesf = E(nc.sbuf_tensor("onesf", [1, 128], F32))
        zcol = E(nc.sbuf_tensor("zcol", [1, 128], BF16))
        wsq = E(nc.sbuf_tensor("wsq", [128, 64], F32))
        rowf = E(nc.sbuf_tensor("rowf", [1, 260], F32))
        afcs = E(nc.sbuf_tensor("afcs", [16, 128], BF16))
        aprs = E(nc.sbuf_tensor("aprs", [16, 128], BF16))
        bls = E(nc.sbuf_tensor("bls", [16, 512], BF16))
        idn = E(nc.sbuf_tensor("idn", [128, 128], BF16))

        B_KT, B_V, B_QT, B_AO, B_QB, B_EB, B_XT = \
            0, 8192, 16384, 24576, 28672, 29696, 31744

        PS_S0 = E(nc.psum_tensor("PS_S0", [128, 1024], F32))
        PS_S1 = E(nc.psum_tensor("PS_S1", [128, 1024], F32))
        PS_O0 = E(nc.psum_tensor("PS_O0", [128, 512], F32))
        PS_O1 = E(nc.psum_tensor("PS_O1", [128, 512], F32))
        PS_W = E(nc.psum_tensor("PS_W", [128, 512], F32))
        PS_TR = E(nc.psum_tensor("PS_TR", [128, 1024], BF16))

        sP = Sem(E(nc.semaphore("sP")), "sP", "tensor")
        sV = Sem(E(nc.semaphore("sV")), "sV", "vector")
        sA = Sem(E(nc.semaphore("sA")), "sA", "scalar")
        sM = Sem(E(nc.semaphore("sM")), "sM", "gpsimd")
        lanes = {"sync": [Sem(E(nc.semaphore(f"sDL{i}")), f"sDL{i}", "sync",
                              is_dma=True) for i in range(14)],
                 "gpsimd": [Sem(E(nc.semaphore(f"sGL{i}")), f"sGL{i}",
                                "gpsimd", is_dma=True) for i in range(6)]}
        lane_i = {"sync": 0, "gpsimd": 0}
        sE = {k: Sem(E(nc.semaphore("sE" + k[0])), "sE" + k[0], k)
              for k in ("vector", "scalar", "gpsimd")}
        eN = {k: [0] for k in sE}

        def ed(e, ins, eng="vector"):
            c = eN[eng]
            c[0] += 1
            ins.then_inc(sE[eng].h, 1)
            e.wait_ge(sE[eng].h, c[0])
            return ins

        prog = Prog()

        def dma(eng, dst, src, waits=(), **kw):
            ls = lanes[eng]
            L = ls[lane_i[eng] % len(ls)]
            lane_i[eng] += 1
            pre = (L, L.n) if L.n > 0 else None
            return prog.step(eng,
                             lambda e: e.dma_start(out=dst, in_=src, **kw),
                             waits=list(waits) + [pre], inc=(L, 16))

        def all_dma_done(eng=None):
            toks = []
            for k, ls in lanes.items():
                if eng is None or k == eng:
                    toks += [(L, L.n) for L in ls if L.n > 0]
            return toks

        t_nw = dma("sync", nws[:, 0:8], nw1T[:, :])
        t_nw2 = dma("sync", nws[:, 8:16], nw2T[:, :])
        dma("sync", bfcs[:, :], bfcT[:, :])
        t_rope = [dma("sync", rcos[:, :], cosg[:, :]),
                  dma("sync", rsin[:, :], sing[:, :])]


        def gp_setup(g):
            ed(g, g.memset(idn[:], 0.0), "gpsimd")
            g.affine_select(out=idn[:], in_=idn[:], compare_op=ALU.not_equal,
                            fill=1.0, base=0, pattern=[[-1, 128]],
                            channel_multiplier=1)
            g.memset(onesf[:, :], 1.0)
            g.memset(zcol[:, :], 0.0)
            v3 = arB[:, B_V:B_V + 8192].rearrange("p (k c) -> p k c", c=128)
            g.memset(v3[:, :, 64:128], 1.0)
            g.memset(smf[:, 47:48], LN_EPS)
            return g.memset(smf[:, 48:49], QK_EPS)
        t_gpset = prog.step("gpsimd", gp_setup, inc=sM)

        def w_chunked(src):
            return src.rearrange("(c p) n -> p c n", p=128)

        def dma_w(dst2d, src, waits=(), ncol=None):
            s3 = src.rearrange("(c p) n -> p c n", p=128)
            n = s3.shape[2]
            d3 = dst2d.rearrange("p (c n) -> p c n", n=n)
            return dma("gpsimd", d3, s3, waits=waits)

        def load_glob_weights(waits=()):
            tw0 = dma_w(wqs[:, :], wq_g, waits)
            tw1 = dma_w(wkv[:, 0:2048], kvd, waits)
            tw2 = dma_w(wkv[:, 2048:2560], kup, waits)
            tw3 = dma_w(wkv[:, 2560:3072], vup, waits)
            tw4 = dma_w(wos[:, :], gwo, waits)
            def scale_rows(v):
                last = None
                for ic in range(8):
                    last = v.tensor_scalar(wqs[:, ts(ic, 1024)],
                                           wqs[:, ts(ic, 1024)],
                                           nws[:, ic:ic + 1], None, ALU.mult)
                for ic in range(8):
                    last = v.tensor_scalar(wkv[:, ts(ic, 256)],
                                           wkv[:, ts(ic, 256)],
                                           nws[:, ic:ic + 1], None, ALU.mult)
                return last
            tq = prog.step("vector", scale_rows, waits=[tw0, tw1, t_nw], inc=sV)
            return {"wq": tq, "wkv": max_toks([tw2, tw3, tq]),
                    "wo": tw4, "all": max_toks([tq, tw2, tw3, tw4])}

        def load_loc_weights(waits=()):
            tw0 = dma_w(wqs[:, :], lwq, waits)
            tw1 = dma_w(wkv[:, 0:2048], lwk, waits)
            tw2 = dma_w(wkv[:, 2048:4096], lwv, waits)
            tw3 = dma_w(wos[:, :], lwo, waits)
            def scale_rows(v):
                last = None
                for ic in range(8):
                    last = v.tensor_scalar(wqs[:, ts(ic, 1024)],
                                           wqs[:, ts(ic, 1024)],
                                           nws[:, 8 + ic:9 + ic], None, ALU.mult)
                for ic in range(16):
                    last = v.tensor_scalar(wkv[:, ts(ic, 256)],
                                           wkv[:, ts(ic, 256)],
                                           nws[:, 8 + ic % 8:9 + ic % 8],
                                           None, ALU.mult)
                return last
            tq = prog.step("vector", scale_rows,
                           waits=[tw0, tw1, tw2, t_nw2], inc=sV)
            return {"wq": tq, "wkv": [tq], "wo": tw3,
                    "all": max_toks([tq, tw3])}

        def ps_init(v):
            v.memset(PS_S0[:, :], 0.0)
            return v.memset(PS_S1[:, :], 0.0)
        t_psinit = prog.step("vector", ps_init, inc=sV)

        g_psw = [t_gpset]
        g_pso = [None, None]
        g_pss = [t_psinit, t_psinit]
        g_ptr = [None]
        g_eb = [None, None]
        g_qt = [None, None]
        g_ao = [None]
        g_xi = [None, None]
        g_ssb = [None, None]
        g_wsq = [None]
        tok_xn = [None] * NT
        t_wq_rd = [None]
        t_wkv_rd = [None]
        t_wo_rd = [None]
        t_rope_rd = [None]

        def emit_wbuild(tt, waits=()):
            def wb(v, tt=tt):
                c = rcos[:, ts(tt, 32)]
                s = rsin[:, ts(tt, 32)]
                v.tensor_mul(wsq[:, 0:32], c, c)
                ed(v, v.tensor_mul(wsq[:, 32:64], s, s))
                return v.tensor_add(wsq[:, 0:32], wsq[:, 0:32], wsq[:, 32:64])
            return prog.step("vector", wb,
                             waits=list(waits) + t_rope + [g_wsq[0]], inc=sV)

        def wsq_view(nheads):
            return wsq[:, 0:32].rearrange("p (a k) -> p a k", a=1)\
                .rearrange("p a k -> p a b k" if False else "p a k -> p a k")

        def transpose_group(src_fn, dst_fn, nchunks, waits=(), identity=None,
                            pst=None, pwidth=128, ev_eng="vector"):
            identity = idn if identity is None else identity
            pst = PS_TR if pst is None else pst
            def tp(t):
                last = None
                for c in range(nchunks):
                    last = t.transpose(pst[:, c * pwidth:(c + 1) * pwidth],
                                       src_fn(c), identity[:])
                return last
            ptok = prog.step("tensor", tp, waits=list(waits) + [g_ptr[0]], inc=sP)
            def ev(e):
                last = None
                for c in range(nchunks):
                    if ev_eng == "vector":
                        last = e.tensor_copy(dst_fn(c),
                                             pst[:, c * pwidth:(c + 1) * pwidth])
                    else:
                        last = e.copy(dst_fn(c),
                                      pst[:, c * pwidth:(c + 1) * pwidth])
                return last
            etok = prog.step(ev_eng, ev, waits=[ptok],
                             inc=sV if ev_eng == "vector" else sA)
            g_ptr[0] = etok
            return ptok, etok

        import os as _os
        _p0n = int(_os.environ.get("P0N", NT))
        if "p0" in phases:
            for T in range(_p0n):
                sq_i, t_i = divmod(T, NTS)
                slot = T % 2
                tld = dma("sync", xi[:, ts(slot, 1024)],
                          x_in[sq_i, ts(t_i, 128), :], waits=[g_xi[slot]])
                def p0_stats(s, slot=slot):
                    return s.activation(scr1[:, :], xi[:, ts(slot, 1024)],
                                        AF.Square, accum_out=smf[:, 49:50])
                ta0 = prog.step("scalar", p0_stats, waits=[tld, t_gpset],
                                inc=sA)
                def p0_stats2(s, slot=slot):
                    return s.activation(scr1[:, :], xi[:, ts(slot, 1024)],
                                        AF.Identity, accum_out=smf[:, 46:47])
                ta = prog.step("scalar", p0_stats2, waits=[ta0], inc=sA)
                def p0_small(v, T=T):
                    ed(v, v.tensor_scalar(stats[:, 4 * T:4 * T + 1],
                                          smf[:, 46:47], 1.0 / D, None,
                                          ALU.mult))
                    v.tensor_scalar(smf[:, 50:51], smf[:, 49:50], 1.0 / D,
                                    None, ALU.mult)
                    ed(v, v.tensor_mul(smf[:, 51:52],
                                       stats[:, 4 * T:4 * T + 1],
                                       stats[:, 4 * T:4 * T + 1]))
                    return v.tensor_sub(smf[:, 50:51], smf[:, 50:51],
                                        smf[:, 51:52])
                tv = prog.step("vector", p0_small, waits=[ta], inc=sV)
                def p0_ln(s, T=T):
                    return s.activation(smf[:, 51:52], smf[:, 50:51], AF.Ln,
                                        bias=smf[:, 47:48])
                ta1 = prog.step("scalar", p0_ln, waits=[tv], inc=sA)
                def p0_exp(s, T=T):
                    return s.activation(stats[:, 4 * T + 1:4 * T + 2],
                                        smf[:, 51:52], AF.Exp, scale=-0.5)
                ta2 = prog.step("scalar", p0_exp, waits=[ta1], inc=sA)
                def p0_mb(v, T=T):
                    return v.tensor_scalar(smf[:, 46:47],
                                           stats[:, 4 * T:4 * T + 1],
                                           stats[:, 4 * T + 1:4 * T + 2], -1.0,
                                           ALU.mult, ALU.mult)
                tv2 = prog.step("vector", p0_mb, waits=[ta2], inc=sV)
                def p0_apply(s, T=T, slot=slot):
                    return s.activation(arA[:, ts(T, 1024)],
                                        xi[:, ts(slot, 1024)], AF.Identity,
                                        scale=stats[:, 4 * T + 1:4 * T + 2],
                                        bias=smf[:, 46:47])
                tok_xn[T] = prog.step("scalar", p0_apply, waits=[tv2], inc=sA)
                g_xi[slot] = tok_xn[T]
            if "xn" in dbg:
                for T in range(_p0n):
                    dma("sync", dbg["xn"][ts(T, 128), :], arA[:, ts(T, 1024)],
                        waits=[tok_xn[T]])
            if "stats" in dbg:
                dma("sync", dbg["stats"][:, :], stats[:, :],
                    waits=[tok_xn[_p0n - 1]])
            if "smf" in dbg:
                dma("sync", dbg["smf"][:, :], smf[:, :],
                    waits=[tok_xn[_p0n - 1]])

        wprep_units = []
        wprep_last_store = [None]

        def wprep_build():
            units = []
            def init_amax(v):
                v.memset(smf[:, 44:45], 0.0)
                return v.memset(smf[:, 45:46], 0.0)
            units.append(lambda: prog.step("vector", init_amax, inc=sV))
            blocks = [(wfcT, 0, r, c) for r in range(0, D, 128)
                      for c in range(0, 4096, 1024)] + \
                     [(wprT, 1, r, 0) for r in range(0, 4096, 128)]
            for src, wi, r, c in blocks:
                def unit(src=src, wi=wi, r=r, c=c):
                    slot = (r // 128 + c // 1024) % 2
                    tl = dma("sync", xi[:, ts(slot, 1024)],
                             src[r:r + 128, c:c + 1024], waits=[g_xi[slot]])
                    def red(v, slot=slot, wi=wi):
                        ed(v, v.tensor_reduce(smf[:, 53:54],
                                              xi[:, ts(slot, 1024)],
                                              AX.X, ALU.max,
                                              apply_absolute_value=True))
                        ed(v, v.tensor_max(smf[:, 44 + wi:45 + wi],
                                           smf[:, 44 + wi:45 + wi],
                                           smf[:, 53:54]))
                        return v.engine_nop()
                    g_xi[slot] = prog.step("vector", red, waits=[tl], inc=sV)
                units.append(unit)
            scl_done = [None]
            def xp_unit():
                ta = dma("sync", xp_scr[0, :], smf[:, 44:45],
                         waits=[g_xi[0], g_xi[1]])
                tb = dma("sync", xp_scr[1, :], smf[:, 45:46])
                t1 = dma("sync", rowf[0:1, 0:256],
                         xp_scr.rearrange("a b -> (a b)"),
                         waits=[ta, tb])
                t2 = t1
                def mx(v):
                    v.tensor_reduce(rowf[0:1, 256:257], rowf[0:1, 0:128],
                                    AX.X, ALU.max)
                    ed(v, v.tensor_reduce(rowf[0:1, 258:259],
                                          rowf[0:1, 128:256], AX.X, ALU.max))
                    v.tensor_scalar(rowf[0:1, 256:257], rowf[0:1, 256:257],
                                    1.0 / 127, None, ALU.mult)
                    ed(v, v.tensor_scalar(rowf[0:1, 258:259],
                                          rowf[0:1, 258:259], 1.0 / 127,
                                          None, ALU.mult))
                    v.reciprocal(rowf[0:1, 257:258], rowf[0:1, 256:257])
                    return v.reciprocal(rowf[0:1, 259:260],
                                        rowf[0:1, 258:259])
                tv = prog.step("vector", mx, waits=[t1, t2], inc=sV)
                def bc(t):
                    return t.matmul(PS_W[:, 0:4], onesf[0:1, :],
                                    rowf[0:1, 256:260], start=True, stop=True)
                tp = prog.step("tensor", bc, waits=[tv, g_psw[0]], inc=sP)
                def cp(v):
                    return v.tensor_copy(smf[:, 40:44], PS_W[:, 0:4])
                g_psw[0] = prog.step("vector", cp, waits=[tp], inc=sV)
                scl_done[0] = g_psw[0]
            units.append(xp_unit)
            g_wpq = [None]
            scl_tok_ref = scl_done
            for src, wi, r, c in blocks:
                def unit(src=src, wi=wi, r=r, c=c):
                    scrd = w1_scr if wi == 0 else w2_scr
                    slot = (r // 128 + c // 1024) % 2
                    tl = dma("sync", xi[:, ts(slot, 1024)],
                             src[r:r + 128, c:c + 1024], waits=[g_xi[slot]])
                    icol = 41 + 2 * wi
                    sccol = 40 + 2 * wi
                    lastst = None
                    for half in range(2):
                        def quant_a(v, slot=slot, half=half, icol=icol):
                            xs = xi[:, slot * 1024 + half * 512:
                                    slot * 1024 + half * 512 + 512]
                            ed(v, v.tensor_scalar(wp_i[:, :], xs,
                                                  smf[:, icol:icol + 1],
                                                  None, ALU.mult))
                            ed(v, v.scalar_tensor_tensor(
                                xs, xs, smf[:, icol:icol + 1], wp_i[:, :],
                                ALU.mult, ALU.subtract))
                            ed(v, v.scalar_tensor_tensor(
                                wp_c[:, :], xs, 0.5, wp_i[:, :], ALU.is_ge,
                                ALU.add))
                            ed(v, v.tensor_scalar(xs, xs, -0.5, None,
                                                  ALU.is_le))
                            return v.tensor_sub(wp_c[:, :], wp_c[:, :], xs)
                        tqa = prog.step("vector", quant_a,
                                        waits=[tl, scl_tok_ref[0]], inc=sV)
                        def quant_b(v, sccol=sccol):
                            return v.tensor_scalar(wp_q[:, :], wp_c[:, :],
                                                   smf[:, sccol:sccol + 1],
                                                   None, ALU.mult)
                        tq = prog.step("vector", quant_b,
                                       waits=[tqa, g_wpq[0]], inc=sV)
                        tst = dma("sync",
                                  scrd[r:r + 128, c + half * 512:
                                       c + half * 512 + 512],
                                  wp_q[:, :], waits=[tq])
                        g_wpq[0] = tst
                        lastst = tst
                    g_xi[slot] = lastst
                    wprep_last_store[0] = None
                units.append(unit)
            return units

        if "p5" in phases:
            wprep_units = wprep_build()
        wprep_pos = [0]

        def wprep_pump(n=1):
            for _ in range(n):
                if wprep_pos[0] < len(wprep_units):
                    wprep_units[wprep_pos[0]]()
                    wprep_pos[0] += 1

        def wprep_finish():
            while wprep_pos[0] < len(wprep_units):
                wprep_units[wprep_pos[0]]()
                wprep_pos[0] += 1
            return all_dma_done("sync")

        def emit_kv_build(sq_i, is_glob, wset):
            toks = []
            st = {}

            def uA(tt):
                T = sq_i * NTS + tt
                col = 2 if is_glob else 3
                kps = PS_O0[:, 0:256] if tt % 2 == 0 else PS_S0[:, 0:256]
                vps = PS_O1[:, 0:256] if tt % 2 == 0 else PS_S1[:, 0:256]
                g_k = g_pso if tt % 2 == 0 else g_pss
                def kv_ss(s, T=T):
                    return s.activation(scr1[:, :], arA[:, ts(T, 1024)],
                                        AF.Square, accum_out=smf[:, 49:50])
                ta = prog.step("scalar", kv_ss,
                               waits=[tok_xn[T], g_scr1g[0]], inc=sA)
                g_scr1g[0] = ta
                def kv_ln(s):
                    return s.activation(smf[:, 50:51], smf[:, 49:50], AF.Ln,
                                        scale=1.0 / D, bias=smf[:, 48:49])
                t_rs0 = prog.step("scalar", kv_ln, waits=[ta], inc=sA)
                def kv_exp(s, T=T, col=col):
                    return s.activation(stats[:, 4 * T + col:4 * T + col + 1],
                                        smf[:, 50:51], AF.Exp, scale=-0.5)
                t_rs = prog.step("scalar", kv_exp, waits=[t_rs0], inc=sA)
                _, t_xnT = transpose_group(
                    lambda c, T=T: arA[:, T * 1024 + c * 128:
                                       T * 1024 + (c + 1) * 128],
                    lambda c: arB[:, B_XT + c * 128:B_XT + (c + 1) * 128],
                    8, waits=[tok_xn[T]], ev_eng="scalar")
                if is_glob:
                    def lat(t):
                        last = None
                        for ic in range(8):
                            last = t.matmul(
                                PS_W[:, 0:256],
                                arB[:, B_XT + ic * 128:B_XT + (ic + 1) * 128],
                                wkv[:, ts(ic, 256)],
                                start=(ic == 0), stop=(ic == 7))
                        return last
                    t_lat = prog.step("tensor", lat,
                                      waits=[t_xnT, g_psw[0]] + wset["wkv"],
                                      inc=sP)
                    t_wkv_rd[0] = t_lat
                    def latev(s, T=T):
                        return s.activation(latb[:, 0:256], PS_W[:, 0:256],
                                            AF.Identity,
                                            scale=stats[:, 4 * T + 2:
                                                        4 * T + 3])
                    t_latev = prog.step("scalar", latev, waits=[t_lat, t_rs],
                                        inc=sA)
                    g_psw[0] = t_latev
                    _, t_latT = transpose_group(
                        lambda c: latb[:, c * 128:(c + 1) * 128],
                        lambda c: arB[:, B_QB + c * 128:B_QB + (c + 1) * 128],
                        2, waits=[t_latev])
                    def kmm(t, kps=kps):
                        last = None
                        for lc in range(2):
                            last = t.matmul(
                                kps,
                                arB[:, B_QB + lc * 128:B_QB + (lc + 1) * 128],
                                wkv[:, 2048 + lc * 256:2048 + (lc + 1) * 256],
                                start=(lc == 0), stop=(lc == 1))
                        return last
                    t_k = prog.step("tensor", kmm, waits=[t_latT, g_k[0]],
                                    inc=sP)
                    def vmm(t, vps=vps):
                        last = None
                        for lc in range(2):
                            last = t.matmul(
                                vps,
                                arB[:, B_QB + lc * 128:B_QB + (lc + 1) * 128],
                                wkv[:, 2560 + lc * 256:2560 + (lc + 1) * 256],
                                start=(lc == 0), stop=(lc == 1))
                        return last
                    t_v = prog.step("tensor", vmm, waits=[t_latT, g_k[1]],
                                    inc=sP)
                    t_wkv_rd[0] = t_v
                else:
                    def kmm(t, kps=kps):
                        last = None
                        for ic in range(8):
                            last = t.matmul(
                                kps,
                                arB[:, B_XT + ic * 128:B_XT + (ic + 1) * 128],
                                wkv[:, ts(ic, 256)],
                                start=(ic == 0), stop=(ic == 7))
                        return last
                    t_k = prog.step("tensor", kmm,
                                    waits=[t_xnT, g_k[0]] + wset["wkv"],
                                    inc=sP)
                    def vmm(t, vps=vps):
                        last = None
                        for ic in range(8):
                            last = t.matmul(
                                vps,
                                arB[:, B_XT + ic * 128:B_XT + (ic + 1) * 128],
                                wkv[:, 2048 + ic * 256:2048 + (ic + 1) * 256],
                                start=(ic == 0), stop=(ic == 7))
                        return last
                    t_v = prog.step("tensor", vmm,
                                    waits=[t_xnT, g_k[1]] + wset["wkv"],
                                    inc=sP)
                    t_wkv_rd[0] = t_v
                st[tt] = (T, kps, vps, g_k, t_k, t_v, t_rs)

            def uB(tt):
                T, kps, vps, g_k, t_k, t_v, t_rs = st.pop(tt)
                t_wb = emit_wbuild(tt)
                col = 2 if is_glob else 3
                v_scale_col = None if is_glob else 3
                k_pre_col = None if is_glob else 3
                def vev(s, tt=tt, T=T, v_scale_col=v_scale_col, vps=vps):
                    dst = arB[:, B_V + tt * 512:B_V + (tt + 1) * 512]
                    d3 = dst.rearrange("p (k c) -> p k c", c=128)[:, :, 0:64]
                    s3 = vps.rearrange("p (k c) -> p k c", c=64)
                    if v_scale_col is None:
                        return s.activation(d3, s3, AF.Identity)
                    return s.activation(d3, s3, AF.Identity,
                                        scale=stats[:, 4 * T + v_scale_col:
                                                    4 * T + v_scale_col + 1])
                t_vev = prog.step("scalar", vev, waits=[t_v, t_rs], inc=sA)
                g_k[1] = t_vev
                def ksq(s, kps=kps):
                    return s.activation(scr2[:, 0:256], kps, AF.Square)
                t_ksq = prog.step("scalar", ksq, waits=[t_k, g_scr2q[0]],
                                  inc=sA)
                def kwm(v):
                    s4 = scr2[:, 0:256].rearrange("p (g b k) -> p g b k",
                                                  g=4, b=2)
                    w4 = wsq[:, 0:32].rearrange("p (a b k) -> p a b k",
                                                a=1, b=1)\
                        .broadcast_to([128, 4, 2, 32])
                    ed(v, v.tensor_mul(s4, s4, w4))
                    return v.tensor_reduce(
                        smf[:, 32:36].rearrange("p (g o) -> p g o", o=1),
                        scr2[:, 0:256].rearrange("p (g k) -> p g k", k=64),
                        AX.X, ALU.add)
                t_kred = prog.step("vector", kwm, waits=[t_ksq, t_wb], inc=sV)
                g_scr2q[0] = t_kred
                g_wsq[0] = t_kred
                if k_pre_col is None:
                    def krs0(s):
                        return s.activation(smf[:, 36:40], smf[:, 32:36],
                                            AF.Ln, scale=1.0 / HD,
                                            bias=smf[:, 48:49])
                    tk0 = prog.step("scalar", krs0, waits=[t_kred], inc=sA)
                    def krs1(s):
                        return s.activation(smf[:, 36:40], smf[:, 36:40],
                                            AF.Exp, scale=-0.5)
                    t_krs = prog.step("scalar", krs1, waits=[tk0], inc=sA)
                else:
                    def krs_pre(v, T=T):
                        return v.tensor_scalar(
                            smf[:, 54:55],
                            stats[:, 4 * T + 3:4 * T + 4],
                            stats[:, 4 * T + 3:4 * T + 4],
                            1.0 / HD, ALU.mult, ALU.mult)
                    tk1 = prog.step("vector", krs_pre, waits=[t_rs], inc=sV)
                    def krsa(s):
                        return s.activation(smf[:, 36:40], smf[:, 32:36],
                                            AF.Ln, scale=smf[:, 54:55],
                                            bias=smf[:, 48:49])
                    tka = prog.step("scalar", krsa, waits=[tk1, t_kred],
                                    inc=sA)
                    def krsb(s):
                        return s.activation(smf[:, 36:40], smf[:, 36:40],
                                            AF.Exp, scale=-0.5)
                    t_krs = prog.step("scalar", krsb, waits=[tka], inc=sA)
                def krope(v, tt=tt, T=T, kps=kps, k_pre_col=k_pre_col):
                    p3 = kps.rearrange("p (g k) -> p g k", k=64)
                    pe, po = p3[:, :, 0:32], p3[:, :, 32:64]
                    c1 = rcos[:, ts(tt, 32)].rearrange("p (o k) -> p o k",
                                                       o=1)\
                        .broadcast_to([128, 4, 32])
                    s1 = rsin[:, ts(tt, 32)].rearrange("p (o k) -> p o k",
                                                       o=1)\
                        .broadcast_to([128, 4, 32])
                    sa = scr2[:, 0:256].rearrange("p (g k) -> p g k", k=64)
                    sb = scr2[:, 256:512].rearrange("p (g k) -> p g k", k=64)
                    e1, e2 = sa[:, :, 0:32], sa[:, :, 32:64]
                    o1, o2 = sb[:, :, 0:32], sb[:, :, 32:64]
                    v.tensor_mul(o1, pe, s1)
                    v.tensor_mul(o2, po, c1)
                    v.tensor_mul(e1, pe, c1)
                    ed(v, v.tensor_mul(e2, po, s1))
                    v.tensor_add(o1, o1, o2)
                    ed(v, v.tensor_sub(e1, e1, e2))
                    kb3 = kbf[:, 0:256].rearrange("p (g k) -> p g k", k=64)
                    r3 = smf[:, 36:40].rearrange("p (g o) -> p g o", o=1)\
                        .broadcast_to([128, 4, 32])
                    if k_pre_col is None:
                        v.tensor_mul(kb3[:, :, 0:32], e1, r3)
                        return v.tensor_mul(kb3[:, :, 32:64], o1, r3)
                    v.scalar_tensor_tensor(
                        kb3[:, :, 0:32], e1,
                        stats[:, 4 * T + 3:4 * T + 4], r3,
                        ALU.mult, ALU.mult)
                    return v.scalar_tensor_tensor(
                        kb3[:, :, 32:64], o1,
                        stats[:, 4 * T + 3:4 * T + 4], r3,
                        ALU.mult, ALU.mult)
                t_krope = prog.step("vector", krope,
                                    waits=[t_krs, t_ksq, g_kbf[0]] + t_rope,
                                    inc=sV)
                t_rope_rd[0] = t_krope
                g_k[0] = t_krope
                g_scr2q[0] = t_krope
                def ktp(t):
                    t.transpose(PS_TR[:, 0:128], kbf[:, 0:128], idn[:])
                    return t.transpose(PS_TR[:, 128:256], kbf[:, 128:256],
                                       idn[:])
                t_ktp = prog.step("tensor", ktp, waits=[t_krope, g_ptr[0]],
                                  inc=sP)
                g_kbf[0] = t_ktp
                def ktev(v, tt=tt):
                    last = None
                    for pair in range(2):
                        for half in range(2):
                            kv_i = pair * 2 + half
                            srcp = PS_TR[64 * half:64 * half + 64,
                                         pair * 128:(pair + 1) * 128]
                            dcol = B_KT + kv_i * 2048 + tt * 128
                            v.tensor_copy(arB[0:64, dcol:dcol + 128], srcp)
                            last = v.tensor_copy(arB[64:128, dcol:dcol + 128],
                                                 srcp)
                    return last
                t_ktev = prog.step("vector", ktev, waits=[t_ktp], inc=sV)
                g_ptr[0] = t_ktev
                toks.append(t_ktev)
                toks.append(t_vev)
                wprep_pump(2)

            for tt in range(NTS):
                uA(tt)
                if tt >= 1:
                    uB(tt - 1)
            uB(NTS - 1)
            return toks

        g_scr2q = [None]
        g_scr1g = [None]
        g_rsq = [None]
        g_kbf = [None]

        def emit_qbuild_units(sq_i, qb, is_glob, wset):
            slot = qb % 2
            rk_col = 2 if is_glob else 3
            units = []
            state = {}

            def u_tile(tq):
                tt = qb * 4 + tq
                T = sq_i * NTS + tt
                _, t_xnT = transpose_group(
                    lambda c, T=T: arA[:, T * 1024 + c * 128:
                                       T * 1024 + (c + 1) * 128],
                    lambda c: arB[:, B_XT + c * 128:B_XT + (c + 1) * 128],
                    8, waits=[tok_xn[T]])
                t_wbq = emit_wbuild(tt)
                def rsq(v, T=T, rk_col=rk_col):
                    return v.tensor_scalar(
                        smf[:, 52:53],
                        stats[:, 4 * T + rk_col:4 * T + rk_col + 1],
                        stats[:, 4 * T + rk_col:4 * T + rk_col + 1],
                        1.0 / HD, ALU.mult, ALU.mult)
                t_rsq = prog.step("vector", rsq,
                                  waits=[t_wbq, g_rsq[0]], inc=sV)
                state["xnT"] = t_xnT
                state["wbq"] = t_wbq
                state["rsq"] = t_rsq

            def u_half_a(tq, hf):
                tt = qb * 4 + tq
                T = sq_i * NTS + tt
                t_xnT = state["xnT"]
                t_wbq = state["wbq"]
                t_rsq = state["rsq"]
                def qproj(t, hf=hf):
                    last = None
                    for ic in range(8):
                        last = t.matmul(
                            PS_W[:, :],
                            arB[:, B_XT + ic * 128:B_XT + (ic + 1) * 128],
                            wqs[:, ic * 1024 + hf * 512:
                                ic * 1024 + hf * 512 + 512],
                            start=(ic == 0), stop=(ic == 7))
                    return last
                t_qp = prog.step("tensor", qproj,
                                 waits=[t_xnT, wset["wq"], g_psw[0]], inc=sP)
                t_wq_rd[0] = t_qp
                def qsq(s):
                    return s.activation(scr2[:, 0:512], PS_W[:, :], AF.Square)
                t_qsq = prog.step("scalar", qsq,
                                  waits=[t_qp, g_scr2q[0]], inc=sA)
                def qwm(v):
                    s4 = scr2[:, 0:512].rearrange("p (g b k) -> p g b k",
                                                  g=8, b=2)
                    w4 = wsq[:, 0:32].rearrange("p (a b k) -> p a b k",
                                                a=1, b=1)\
                        .broadcast_to([128, 8, 2, 32])
                    ed(v, v.tensor_mul(s4, s4, w4))
                    return v.tensor_reduce(
                        smf[:, 0:8].rearrange("p (g o) -> p g o", o=1),
                        scr2[:, 0:512].rearrange("p (g k) -> p g k", k=64),
                        AX.X, ALU.add)
                t_qred = prog.step("vector", qwm, waits=[t_qsq, t_wbq], inc=sV)
                g_scr2q[0] = t_qred
                g_wsq[0] = t_qred
                state[("h", tq, hf)] = (t_qred, t_qsq)

            def u_half_b(tq, hf):
                tt = qb * 4 + tq
                T = sq_i * NTS + tt
                t_rsq = state["rsq"]
                (t_qred, t_qsq) = state.pop(("h", tq, hf))

                def qln(s):
                    return s.activation(smf[:, 8:16], smf[:, 0:8], AF.Ln,
                                        scale=smf[:, 52:53],
                                        bias=smf[:, 48:49])
                t_qln = prog.step("scalar", qln, waits=[t_qred, t_rsq],
                                  inc=sA)
                g_rsq[0] = t_qln
                def qexp(s):
                    return s.activation(smf[:, 8:16], smf[:, 8:16], AF.Exp,
                                        scale=-0.5)
                t_qrs = prog.step("scalar", qexp, waits=[t_qln], inc=sA)
                def qrope(v, tt=tt, hf=hf, T=T, rk_col=rk_col):
                    p3 = PS_W[:, :].rearrange("p (g k) -> p g k", k=64)
                    pe, po = p3[:, :, 0:32], p3[:, :, 32:64]
                    c1 = rcos[:, ts(tt, 32)]\
                        .rearrange("p (o k) -> p o k", o=1)\
                        .broadcast_to([128, 8, 32])
                    s1 = rsin[:, ts(tt, 32)]\
                        .rearrange("p (o k) -> p o k", o=1)\
                        .broadcast_to([128, 8, 32])
                    sa = scr1[:, 0:512].rearrange("p (g k) -> p g k", k=64)
                    sb = scr1[:, 512:1024].rearrange("p (g k) -> p g k", k=64)
                    e1, e2 = sa[:, :, 0:32], sa[:, :, 32:64]
                    o1, o2 = sb[:, :, 0:32], sb[:, :, 32:64]
                    v.tensor_mul(o1, pe, s1)
                    v.tensor_mul(o2, po, c1)
                    v.tensor_mul(e1, pe, c1)
                    ed(v, v.tensor_mul(e2, po, s1))
                    v.tensor_add(o1, o1, o2)
                    ed(v, v.tensor_sub(e1, e1, e2))
                    qb3 = arB[:, B_QB + hf * 512:B_QB + hf * 512 + 512]\
                        .rearrange("p (g k) -> p g k", k=64)
                    rq3 = smf[:, 8:16].rearrange("p (g o) -> p g o", o=1)\
                        .broadcast_to([128, 8, 32])
                    v.scalar_tensor_tensor(
                        qb3[:, :, 0:32], e1,
                        stats[:, 4 * T + rk_col:4 * T + rk_col + 1], rq3,
                        ALU.mult, ALU.mult)
                    return v.scalar_tensor_tensor(
                        qb3[:, :, 32:64], o1,
                        stats[:, 4 * T + rk_col:4 * T + rk_col + 1], rq3,
                        ALU.mult, ALU.mult)
                t_qrope = prog.step("vector", qrope,
                                    waits=[t_qrs, t_qsq] + t_rope, inc=sV)
                t_rope_rd[0] = t_qrope
                g_psw[0] = t_qrope
                g_scr1g[0] = t_qrope
                _, t_qtev = transpose_group(
                    lambda c, hf=hf: arB[:, B_QB + hf * 512 + c * 128:
                                         B_QB + hf * 512 + (c + 1) * 128],
                    lambda c, hf=hf, slot=slot, tq=tq: arB[
                        :, B_QT + slot * 4096 + (hf * 4 + c) * 512 +
                        tq * 128:
                        B_QT + slot * 4096 + (hf * 4 + c) * 512 +
                        tq * 128 + 128],
                    4, waits=[t_qrope, g_qt[slot]])
                qb_ready[slot] = t_qtev
                wprep_pump(1)

            for tq in range(4):
                units.append(lambda tq=tq: u_tile(tq))
                for hf in range(2):
                    units.append(lambda tq=tq, hf=hf: u_half_a(tq, hf))
                    units.append(lambda tq=tq, hf=hf: u_half_b(tq, hf))
            return units

        qb_ready = [None, None]

        def emit_attn(sq_i, is_glob, wset, t_kv_bar):
            S_T = [PS_S0, PS_S1]
            for u in emit_qbuild_units(sq_i, 0, is_glob, wset):
                u()
            for qb in range(4):
                q0 = qb * 512
                t_qt = qb_ready[qb % 2]
                pending = (emit_qbuild_units(sq_i, qb + 1, is_glob, wset)
                           if qb < 3 else [])
                pend_i = [0]
                def pump_q(n=2):
                    for _ in range(n):
                        if pend_i[0] < len(pending):
                            pending[pend_i[0]]()
                            pend_i[0] += 1
                ep_last = None
                last_qk = None
                if is_glob:
                    kts = list(range(0, 4 * qb + 4))
                else:
                    kts = [k for k in range(4 * qb - 2, 4 * qb + 4) if k >= 0]
                n_kt = len(kts)
                for hp in range(HP):
                    kv_i = hp // 2
                    qk_toks = {}
                    exp_toks = {}
                    geom = []
                    for kt in kts:
                        k0 = kt * 128
                        q_s = max(q0, k0)
                        q_e = q0 + 512 if is_glob else min(k0 + 384, q0 + 512)
                        geom.append((kt, k0, q_s, q_e - q_s, q_s - q0))
                    def emit_qk(i):
                        kt, k0, q_s, w, q_off = geom[i]
                        r = i % 2
                        def qk(t, hp=hp, kt=kt, kv_i=kv_i, r=r, q_off=q_off,
                               w=w, slot=qb % 2):
                            S = S_T[r]
                            lastm = None
                            for hh in range(2):
                                b = hh * 64
                                lastm = t.matmul(
                                    S[:, hh * 512:hh * 512 + w],
                                    arB[b:b + 64,
                                        B_KT + kv_i * 2048 + kt * 128:
                                        B_KT + kv_i * 2048 + (kt + 1) * 128],
                                    arB[b:b + 64,
                                        B_QT + slot * 4096 + hp * 512 + q_off:
                                        B_QT + slot * 4096 + hp * 512 +
                                        q_off + w],
                                    start=True, stop=True)
                            return lastm
                        qk_toks[i] = prog.step(
                            "tensor", qk, waits=[t_qt, t_kv_bar, g_pss[r]],
                            inc=sP)
                    emit_qk(0)
                    for i in range(n_kt):
                        kt, k0, q_s, w, q_off = geom[i]
                        r = i % 2
                        er = i % 2
                        def expf(s, r=r, er=er, w=w):
                            if w == 512:
                                return s.activation(
                                    arB[:, B_EB + er * 1024:
                                        B_EB + er * 1024 + 512 + w],
                                    S_T[r][:, 0:512 + w], AF.Exp, scale=0.125)
                            s.activation(
                                arB[:, B_EB + er * 1024:
                                    B_EB + er * 1024 + w],
                                S_T[r][:, 0:w], AF.Exp, scale=0.125)
                            return s.activation(
                                arB[:, B_EB + er * 1024 + 512:
                                    B_EB + er * 1024 + 512 + w],
                                S_T[r][:, 512:512 + w], AF.Exp, scale=0.125)
                        t_exp = prog.step("scalar", expf,
                                          waits=[qk_toks[i], g_eb[er]],
                                          inc=sA)
                        g_pss[r] = t_exp
                        exp_toks[i] = t_exp
                        if i + 1 < n_kt:
                            emit_qk(i + 1)
                        base = q_s - k0
                        need_c = base < 128
                        need_u = (not is_glob) and (base + w - 1 >= 256)
                        t_msk = None
                        if need_c or need_u:
                            def msk(g, er=er, w=w, base=base, need_c=need_c,
                                    need_u=need_u):
                                last = None
                                for hh in range(2):
                                    off = B_EB + er * 1024 + hh * 512
                                    if need_c:
                                        wd = min(w, 128 - base)
                                        ap = arB[:, off:off + wd]
                                        last = g.affine_select(
                                            out=ap, in_=ap,
                                            compare_op=ALU.is_ge, fill=0.0,
                                            base=base, pattern=[[1, wd]],
                                            channel_multiplier=-1)
                                        if need_u:
                                            ed(g, last, "gpsimd")
                                    if need_u:
                                        lo = max(0, 256 - base - 127)
                                        ap = arB[:, off + lo:off + w]
                                        last = g.affine_select(
                                            out=ap, in_=ap,
                                            compare_op=ALU.is_ge, fill=0.0,
                                            base=255 - base - lo,
                                            pattern=[[-1, w - lo]],
                                            channel_multiplier=1)
                                return last
                            t_msk = prog.step("gpsimd", msk, waits=[t_exp],
                                              inc=sM)
                        if i == 0 and not is_glob:
                            def pvz(t):
                                t.matmul(PS_O0[:, :], zcol[0:1, :],
                                         arB[0:1, B_V:B_V + 512], start=True,
                                         stop=False)
                                return t.matmul(PS_O1[:, :], zcol[0:1, :],
                                                arB[0:1, B_V:B_V + 512],
                                                start=True, stop=False)
                            prog.step("tensor", pvz,
                                      waits=[g_pso[0], g_pso[1], t_kv_bar],
                                      inc=sP)
                        def pv(t, kt=kt, kv_i=kv_i, er=er, w=w, q_off=q_off,
                               first=(i == 0 and is_glob),
                               last=(i == n_kt - 1)):
                            lastm = None
                            for hh, PO in ((0, PS_O0), (1, PS_O1)):
                                lastm = t.matmul(
                                    PO[:, q_off:q_off + w],
                                    arB[:, B_V + kt * 512 + kv_i * 128:
                                        B_V + kt * 512 + (kv_i + 1) * 128],
                                    arB[:, B_EB + er * 1024 + hh * 512:
                                        B_EB + er * 1024 + hh * 512 + w],
                                    start=first, stop=last)
                            return lastm
                        w8 = [t_exp, t_msk]
                        if i == 0 and is_glob:
                            w8 += [g_pso[0], g_pso[1]]
                        t_pv = prog.step("tensor", pv, waits=w8, inc=sP)
                        g_eb[er] = t_pv
                        if hp >= 4 and i == n_kt // 2:
                            pump_q(1)
                    last_qk = qk_toks[n_kt - 1]
                    def epi(v, hp=hp):
                        v.reciprocal(rz[:, :], PS_O0[64:128, 0:512])
                        ed(v, v.reciprocal(rz2[:, :], PS_O1[64:128, 0:512]))
                        v.tensor_mul(
                            arB[0:64, B_AO + hp * 512:B_AO + (hp + 1) * 512],
                            PS_O0[0:64, :], rz[:, :])
                        return v.tensor_mul(
                            arB[64:128, B_AO + hp * 512:B_AO + (hp + 1) * 512],
                            PS_O1[0:64, :], rz2[:, :])
                    t_epi = prog.step("vector", epi, waits=[t_pv, g_ao[0]],
                                      inc=sV)
                    g_pso[0] = t_epi
                    g_pso[1] = t_epi
                    ep_last = t_epi
                    pump_q(1)
                    wprep_pump(1)
                g_qt[qb % 2] = last_qk
                wo_gate = [g_pso[0], g_pso[1]]
                widx = 0
                last_wo_pe = None
                for tq in range(4):
                    T = sq_i * NTS + qb * 4 + tq
                    for ob in range(2):
                        PSx = PS_O0 if widx % 2 == 0 else PS_O1
                        def wo(t, tq=tq, ob=ob, PSx=PSx):
                            last = None
                            for ic in range(8):
                                last = t.matmul(
                                    PSx[:, :],
                                    arB[:, B_AO + ic * 512 + tq * 128:
                                        B_AO + ic * 512 + tq * 128 + 128],
                                    wos[:, ic * 1024 + ob * 512:
                                        ic * 1024 + ob * 512 + 512],
                                    start=(ic == 0), stop=(ic == 7))
                            return last
                        t_wo = prog.step("tensor", wo,
                                         waits=[ep_last, wo_gate[widx % 2],
                                                wset["wo"]], inc=sP)
                        t_wo_rd[0] = t_wo
                        last_wo_pe = t_wo
                        if is_glob:
                            def woev(v, T=T, ob=ob, PSx=PSx):
                                dst = arA[:, T * 1024 + ob * 512:
                                          T * 1024 + ob * 512 + 512]
                                return v.tensor_add(dst, PSx[:, :], dst)
                            t_ev = prog.step("vector", woev, waits=[t_wo],
                                             inc=sV)
                        else:
                            def woev(v, T=T, ob=ob, PSx=PSx):
                                src = arA[:, T * 1024 + ob * 512:
                                          T * 1024 + ob * 512 + 512]
                                dsts = ssb[:, (T % 2) * 1024 + ob * 512:
                                           (T % 2) * 1024 + ob * 512 + 512]
                                return v.tensor_add(dsts, PSx[:, :], src)
                            t_ev = prog.step("vector", woev,
                                             waits=[t_wo, g_ssb[T % 2]],
                                             inc=sV)
                        wo_gate[widx % 2] = t_ev
                        if widx % 2 == 0:
                            g_pso[0] = t_ev
                        else:
                            g_pso[1] = t_ev
                        widx += 1
                        pump_q(1)
                        if is_glob:
                            tok_xn[T] = t_ev
                        elif ob == 1:
                            tst = dma("sync", s_scr[ts(T, 128), :],
                                      ssb[:, (T % 2) * 1024:
                                          (T % 2) * 1024 + 1024],
                                      waits=[t_ev])
                            g_ssb[T % 2] = tst
                            if "s" in dbg:
                                g_ssb[T % 2] = dma(
                                    "sync", dbg["s"][ts(T, 128), :],
                                    ssb[:, (T % 2) * 1024:(T % 2) * 1024 + 1024],
                                    waits=[t_ev, tst])
                g_ao[0] = last_wo_pe
                for u in pending[pend_i[0]:]:
                    u()

        wset_g = load_glob_weights() if "p1" in phases else None
        p5_w1_gates = [None, None]
        if "p1" in phases:
            for sq_i in range(NSEQ):
                if sq_i > 0:
                    wset_g = load_glob_weights(
                        waits=[t_wq_rd[0], t_wkv_rd[0], t_wo_rd[0]])
                    t_rope = [dma("sync", rcos[:, :], cosg[:, :],
                                  waits=[t_rope_rd[0]]),
                              dma("sync", rsin[:, :], sing[:, :])]
                kv_toks = emit_kv_build(sq_i, True, wset_g)
                t_bar = prog.barrier("vector", max_toks(kv_toks) + t_rope, sV)
                if "p2" in phases:
                    emit_attn(sq_i, True, wset_g, t_bar)
                    if "x1" in dbg and sq_i == NSEQ - 1:
                        for T in range(NT):
                            dma("sync", dbg["x1"][ts(T, 128), :],
                                arA[:, ts(T, 1024)], waits=[tok_xn[T]])
                if "p3" in phases:
                    wset_l = load_loc_weights(
                        waits=[t_wq_rd[0], t_wkv_rd[0], t_wo_rd[0]])
                    t_rope = [dma("sync", rcos[:, :], cosl[:, :],
                                  waits=[t_rope_rd[0]]),
                              dma("sync", rsin[:, :], sinl[:, :])]
                    kv_toks = emit_kv_build(sq_i, False, wset_l)
                    t_bar2 = prog.barrier("vector",
                                          max_toks(kv_toks) + t_rope, sV)
                    emit_attn(sq_i, False, wset_l, t_bar2)
                if sq_i == 0:
                    p5_w1_gates[0] = max_toks(
                        [t_wq_rd[0], t_wo_rd[0], t_wkv_rd[0],
                         tok_xn[NTS - 1]])
            if "kt" in dbg:
                dma("sync", dbg["kt"][:, :], arB[:, B_KT:B_KT + 8192],
                    waits=[(sV, sV.n)])
            if "vt" in dbg:
                dma("sync", dbg["vt"][:, :], arB[:, B_V:B_V + 8192],
                    waits=[(sV, sV.n), (sA, sA.n)])
            if "qt" in dbg:
                dma("sync", dbg["qt"][:, :], arB[:, B_QT:B_QT + 4096],
                    waits=[(sV, sV.n)])
        if "p5" in phases:
            t_allstore = wprep_finish()
            bar_attn = prog.barrier("vector",
                                    [(sV, sV.n), (sA, sA.n), (sP, sP.n),
                                     (sM, sM.n)], sV)
            t_w1 = []
            for ic in range(8):
                t_w1.append(dma("sync", arA[:, ts(ic, 4096)],
                                w1_scr[ts(ic, 128), :],
                                waits=t_allstore + [bar_attn]))
            t_w2 = []
            for fc in range(32):
                t_w2.append(dma("sync", arB[:, ts(fc, 1024)],
                                w2_scr[ts(fc, 128), :],
                                waits=t_allstore + [bar_attn]))
            bar_bpr = None
            for hb in range(2):
                t_row = dma("sync", scr2[0:1, 0:512],
                            bprR[0:1, ts(hb, 512)],
                            waits=[bar_attn, bar_bpr])
                def bpr_bc(t):
                    return t.matmul(PS_W[:, :], onesf[0:1, :],
                                    scr2[0:1, 0:512], start=True, stop=True)
                tp0 = prog.step("tensor", bpr_bc, waits=[t_row, g_psw[0]],
                                inc=sP)
                def bpr_cp(v, hb=hb):
                    return v.tensor_copy(bprb[:, ts(hb, 512)], PS_W[:, :])
                bar_bpr = prog.step("vector", bpr_cp, waits=[tp0], inc=sV)
                g_psw[0] = bar_bpr

            g_bls = [t_gpset]
            fold1 = []
            t_afc_prev = [None]
            fold_ps = [PS_W[:, :], PS_S0[:, 0:512], PS_S0[:, 512:1024],
                       PS_S1[:, 0:512], PS_S1[:, 512:1024]]
            g_fps = [g_psw[0], g_pss[0], g_pss[0], g_pss[1], g_pss[1]]
            fidx = [0]
            for fb in range(8):
                tb_ = dma("gpsimd", bls[:, :], bfcL[:, ts(fb, 512)],
                          waits=[g_bls[0]])
                for ic in range(8):
                    ta_f = dma("gpsimd", afcs[:, :], afc[:, ts(ic, 128)],
                               waits=[t_afc_prev[0]])
                    j = fidx[0] % 5
                    PSf = fold_ps[j]
                    def lmm(t, PSf=PSf):
                        return t.matmul(PSf, afcs[:, :],
                                        bls[:, :], start=True, stop=True)
                    t_lm = prog.step("tensor", lmm,
                                     waits=[tb_, ta_f, g_fps[j], t_w1[ic]],
                                     inc=sP)
                    t_afc_prev[0] = t_lm
                    def ladd(v, ic=ic, fb=fb, PSf=PSf):
                        dst = arA[:, ic * 4096 + fb * 512:
                                  ic * 4096 + fb * 512 + 512]
                        return v.scalar_tensor_tensor(dst, PSf, 2.0,
                                                      dst, ALU.mult, ALU.add)
                    t_la = prog.step("vector", ladd, waits=[t_lm], inc=sV)
                    g_fps[j] = t_la
                    fidx[0] += 1
                    fold1.append(t_la)
                g_bls[0] = fold1[-1]
            fold2 = []
            t_apr_prev = [None]
            for db in range(2):
                tb_ = dma("gpsimd", bls[:, 0:512], bprL[:, ts(db, 512)],
                          waits=[g_bls[0]])
                for fc in range(32):
                    ta_ = dma("gpsimd", aprs[:, :], apr[:, ts(fc, 128)],
                              waits=[t_apr_prev[0]])
                    j = fidx[0] % 5
                    PSf = fold_ps[j]
                    def lmm(t, PSf=PSf):
                        return t.matmul(PSf, aprs[:, :], bls[:, 0:512],
                                        start=True, stop=True)
                    t_lm = prog.step("tensor", lmm,
                                     waits=[tb_, ta_, g_fps[j], t_w2[fc]],
                                     inc=sP)
                    t_apr_prev[0] = t_lm
                    def ladd(v, fc=fc, db=db, PSf=PSf):
                        dst = arB[:, fc * 1024 + db * 512:
                                  fc * 1024 + db * 512 + 512]
                        return v.scalar_tensor_tensor(dst, PSf, 2.0,
                                                      dst, ALU.mult, ALU.add)
                    t_la = prog.step("vector", ladd, waits=[t_lm], inc=sV)
                    g_fps[j] = t_la
                    fidx[0] += 1
                    fold2.append(t_la)
                g_bls[0] = fold2[-1]
            bar_w1 = prog.barrier("vector", max_toks(fold1), sV)
            bar_w2 = prog.barrier("vector", max_toks(fold2), sV)
            g_psw[0] = bar_w2
            g_pss[0] = bar_w2
            g_pss[1] = bar_w2
            g_pso[0] = bar_w2
            g_pso[1] = bar_w2
            if "w1" in dbg:
                for ic in range(8):
                    dma("sync", dbg["w1"][ts(ic, 128), :], arA[:, ts(ic, 4096)],
                        waits=[bar_w1])
            if "w2" in dbg:
                for fc in range(32):
                    dma("sync", dbg["w2"][ts(fc, 128), :], arB[:, ts(fc, 1024)],
                        waits=[bar_w2])

            W_XT, W_XN = 0, 2048
            g_fc = [None, None, None, None]
            g_hT = [None]
            g_out = [None, None]
            g_z = [None, None]
            g_g = [None]
            pj_ps = [PS_O0, PS_O1, PS_W]
            g_pj = [None, None, g_psw[0]]
            for tb in range(int(_os.environ.get("P5NB", 16))):
                tiles = [tb * 2, tb * 2 + 1]
                t_xnT_all = []
                t_x_ld = {}
                t_s_ld = {}
                for i, T in enumerate(tiles):
                    sq_i, t_i = divmod(T, NTS)
                    slot = T % 2
                    tld = dma("sync", xi[:, ts(slot, 1024)],
                              x_in[sq_i, ts(t_i, 128), :],
                              waits=[g_out[slot], g_xi[slot]])
                    t_x_ld[T] = tld
                    t_s_ld[T] = dma("sync", ssb[:, ts(slot, 1024)],
                                    s_scr[ts(T, 128), :],
                                    waits=[g_ssb[slot], bar_attn])
                    g_ssb[slot] = None
                    def lnr(v, T=T, slot=slot):
                        return v.tensor_scalar(
                            wos[:, W_XN:W_XN + 1024], xi[:, ts(slot, 1024)],
                            stats[:, 4 * T:4 * T + 1],
                            stats[:, 4 * T + 1:4 * T + 2],
                            ALU.subtract, ALU.mult)
                    t_ln = prog.step("vector", lnr,
                                     waits=[tld, bar_w2, g_hT[0],
                                            (t_xnT_all[-1] if t_xnT_all
                                             else None)], inc=sV)
                    g_xi[slot] = t_ln
                    _, t_tx = transpose_group(
                        lambda c: wos[:, W_XN + c * 128:W_XN + (c + 1) * 128],
                        lambda c, i=i: wos[:, W_XT + c * 256 + i * 128:
                                           W_XT + c * 256 + i * 128 + 128],
                        8, waits=[t_ln])
                    t_xnT_all.append(t_tx)
                bar_xnT = prog.barrier("vector", max_toks(t_xnT_all), sV)
                t_gel_last = None
                for f8 in range(32):
                    Sx = PS_S0 if (f8 % 4) < 2 else PS_S1
                    So = (f8 % 2) * 512
                    def fcm(t, f8=f8, Sx=Sx, So=So):
                        last = None
                        for ic in range(8):
                            last = t.matmul(
                                Sx[:, So:So + 256],
                                arA[:, ic * 4096 + f8 * 128:
                                    ic * 4096 + f8 * 128 + 128],
                                wos[:, W_XT + ic * 256:W_XT + ic * 256 + 256],
                                start=(ic == 0), stop=(ic == 7))
                        return last
                    t_fc = prog.step("tensor", fcm,
                                     waits=[bar_xnT, bar_w1, g_fc[f8 % 4]],
                                     inc=sP)
                    def gel(s, f8=f8, Sx=Sx, So=So):
                        return s.activation(
                            wqs[:, ts(f8, 256)], Sx[:, So:So + 256],
                            AF.Gelu, bias=bfcs[:, f8:f8 + 1])
                    t_g = prog.step("scalar", gel, waits=[t_fc, g_hT[0]],
                                    inc=sA)
                    g_fc[f8 % 4] = t_g
                    t_gel_last = t_g
                if tb == 0 and "h0" in dbg:
                    dma("sync", dbg["h0"][:, :], wqs[:, :],
                        waits=[t_gel_last])
                pj_idx = 0
                last_pj = None
                for qt in range(2):
                    T = tiles[qt]
                    slot = T % 2
                    for ob in range(2):
                        psl = pj_idx % 3
                        POap = pj_ps[psl][:, :]
                        def pjm(t, qt=qt, ob=ob, POap=POap):
                            last = None
                            for f8 in range(32):
                                last = t.matmul(
                                    POap,
                                    wqs[:, f8 * 256 + qt * 128:
                                        f8 * 256 + qt * 128 + 128],
                                    arB[:, f8 * 1024 + ob * 512:
                                        f8 * 1024 + ob * 512 + 512],
                                    start=(f8 == 0), stop=(f8 == 31))
                            return last
                        t_pj = prog.step("tensor", pjm,
                                         waits=[t_gel_last, g_pj[psl],
                                                bar_w2], inc=sP)
                        last_pj = t_pj
                        def zadd(v, POap=POap, ob=ob, qt=qt):
                            dst = scr1[:, ts(qt, 512)]
                            return v.tensor_tensor(dst, POap,
                                                   bprb[:, ts(ob, 512)],
                                                   ALU.add)
                        t_z = prog.step("vector", zadd,
                                        waits=[t_pj, bar_bpr, g_z[qt]],
                                        inc=sV)
                        g_pj[psl] = t_z
                        def sil(s, qt=qt):
                            return s.activation(scr2[:, :],
                                                scr1[:, ts(qt, 512)],
                                                AF.Silu)
                        t_si = prog.step("scalar", sil,
                                         waits=[t_z, g_g[0]], inc=sA)
                        g_z[qt] = t_si
                        def comb(v, slot=slot, ob=ob):
                            ed(v, v.tensor_mul(
                                scr2[:, :], scr2[:, :],
                                ssb[:, slot * 1024 + ob * 512:
                                    slot * 1024 + ob * 512 + 512]))
                            dst = xi[:, slot * 1024 + ob * 512:
                                     slot * 1024 + ob * 512 + 512]
                            return v.tensor_add(dst, scr2[:, :], dst)
                        t_cb = prog.step("vector", comb,
                                         waits=[t_si, t_s_ld[T], t_x_ld[T]],
                                         inc=sV)
                        g_g[0] = t_cb
                        if ob == 1:
                            sq_i, t_i = divmod(T, NTS)
                            t_out = dma("sync", out_t[sq_i, ts(t_i, 128), :],
                                        xi[:, ts(slot, 1024)], waits=[t_cb])
                            g_out[slot] = t_out
                            g_xi[slot] = t_cb
                            g_ssb[slot] = t_cb
                        pj_idx += 1
                g_hT[0] = last_pj

        fin = [(L, L.n) for L in lanes["sync"] if L.n > 0] + \
              [(L, L.n) for L in lanes["gpsimd"] if L.n > 0]
        def final_wait(e, _fin=fin):
            last = None
            for L, v in _fin:
                last = e.wait_ge(L.h, v)
            return last
        prog.steps["sync"].append((final_wait, [], None))

        prog.check()
        block = E(nc.Block())
        prog.emit(block)

    return nc, dbg


def _perm_deinterleave(nheads, hd=HD):
    p = []
    for h in range(nheads):
        base = h * hd
        p += [base + 2 * j for j in range(hd // 2)]
        p += [base + 2 * j + 1 for j in range(hd // 2)]
    return np.asarray(p)


def _rope_tables(base):
    import jax
    import jax.numpy as jnp
    with jax.default_device(jax.devices("cpu")[0]):
        inv = 1.0 / base ** (jnp.arange(0, HD, 2, dtype=jnp.float32) / HD)
        fr = jnp.arange(SEQ, dtype=jnp.float32)[:, None] * inv[None, :]
        cos = np.asarray(jnp.cos(fr), dtype=np.float32)
        sin = np.asarray(jnp.sin(fr), dtype=np.float32)
    cos = cos.reshape(NTS, 128, 32).transpose(1, 0, 2).reshape(128, 512)
    sin = sin.reshape(NTS, 128, 32).transpose(1, 0, 2).reshape(128, 512)
    return np.ascontiguousarray(cos), np.ascontiguousarray(sin)


_BUILD_CACHE = {}
TAP_RESULTS = {}


def _get_build(debug=(), phases=("p0", "p1", "p2", "p3", "p5")):
    key = (tuple(sorted(debug)), tuple(phases))
    if key not in _BUILD_CACHE:
        _BUILD_CACHE[key] = build(debug=debug, phases=phases)
    return _BUILD_CACHE[key]


def prep_inputs(inputs):
    f = np.ascontiguousarray
    pq = _perm_deinterleave(H)
    pk = _perm_deinterleave(KVH)
    cg, sg = _rope_tables(1000000.0)
    cl, sl = _rope_tables(10000.0)
    common = {
        "wq_g": f(inputs["L0_attn_g_wq"][:, pq]),
        "kvd": f(inputs["L0_attn_g_kv_down"]),
        "kup": f(inputs["L0_attn_g_k_up"][:, pk]),
        "vup": f(inputs["L0_attn_g_v_up"]),
        "gwo": f(inputs["L0_attn_g_wo"]),
        "lwq": f(inputs["L0_attn_l_wq"][:, pq]),
        "lwk": f(inputs["L0_attn_l_wk"][:, pk]),
        "lwv": f(inputs["L0_attn_l_wv"]),
        "lwo": f(inputs["L0_attn_l_wo"]),
        "nw1T": f(inputs["L0_attn_norm1_w"].reshape(8, 128).T),
        "nw2T": f(inputs["L0_attn_norm2_w"].reshape(8, 128).T),
        "wfcT": f(inputs["L0_ffn_W_fc"].T),
        "wprT": f(inputs["L0_ffn_W_proj"].T),
        "bfcT": f(inputs["L0_ffn_b_fc"].reshape(32, 128).T),
        "bprR": f(inputs["L0_ffn_b_proj"].reshape(1, D)),
        "afc": f(inputs["L0_ffn_A_fc"]),
        "bfcL": f(inputs["L0_ffn_B_fc"].T),
        "apr": f(inputs["L0_ffn_A_proj"]),
        "bprL": f(inputs["L0_ffn_B_proj"].T),
        "cosg": cg, "sing": sg, "cosl": cl, "sinl": sl,
    }
    x = np.asarray(inputs["L0_attn_hidden"])
    in_maps = []
    for c in range(8):
        m = dict(common)
        m["x"] = f(x[2 * c:2 * c + 2])
        in_maps.append(m)
    return in_maps


def kernel(**inputs):
    import os as _os
    taps = tuple(t for t in _os.environ.get("KTAPS", "").split(",") if t)
    nc, _ = _get_build(debug=taps)
    in_maps = prep_inputs(inputs)
    ncores = int(_os.environ.get("KCORES", "8"))
    res = run_bass_kernel_spmd(nc, in_maps[:ncores], core_ids=list(range(ncores)))
    if taps:
        TAP_RESULTS.clear()
        for k in taps:
            TAP_RESULTS[k] = np.stack(
                [np.asarray(r["dbg_" + k]) for r in res.results])
    out = np.concatenate([np.asarray(r["out"]) for r in res.results], axis=0)
    return out.astype(np.float32)
